# version 02 (final)
# baseline (speedup 1.0000x reference)
from contextlib import ExitStack
import numpy as np
import ml_dtypes
import concourse.bass as bass
import concourse.mybir as mybir
from concourse.bass_utils import run_bass_kernel_spmd

F32 = mybir.dt.float32
BF16 = mybir.dt.bfloat16
U32 = mybir.dt.uint32
AF = mybir.ActivationFunctionType
OP = mybir.AluOpType
BF = ml_dtypes.bfloat16
AX = mybir.AxisListType.X

D, S, H, HD, OWN = 1024, 4096, 16, 64, 2048
EPS = 1e-5
NGRP, NBAT, GPB, NSB = 256, 16, 16, 64
ATTN_NBAT = 16
FFN_MODE = 3
FFN_TCN = 4


def perm_rows():
    p = []
    for base in (0, 1024):
        for half in (0, 1):
            for h in range(H):
                for f in range(32):
                    p.append(base + h * HD + half * 32 + f)
    p += list(range(2048, 3072))
    return np.array(p, dtype=np.int64)


def host_consts(rot):
    c = {}
    c["identD"] = np.eye(128, dtype=np.float32).astype(BF)
    p = np.arange(128)
    c["m8D"] = (p[:, None] % 8 == p[None, :] % 8).astype(BF)
    c["e1D"] = (np.arange(16)[:, None] == (p[None, :] // 8)).astype(BF)
    inv = 1.0 / (10000.0 ** (np.arange(0, HD, 2, dtype=np.float64) / HD))
    t = np.arange(S, dtype=np.float64)
    fr = t[None, :] * inv[:, None]
    cos = np.roll(np.cos(fr).astype(np.float32), -rot, axis=1)
    sin = np.roll(np.sin(fr).astype(np.float32), -rot, axis=1)
    c["costabD"] = np.tile(cos, (4, 1)).astype(BF)
    c["sintabD"] = np.tile(sin, (4, 1)).astype(BF)
    return c


class Cnt:
    def __init__(self, h):
        self.h = h
        self.n = 0

    def inc(self, instr, k=1):
        instr.then_inc(self.h, k)
        self.n += k
        return self.n


class Ctx:
    pass


def build_l0(dbg=""):
    nc = bass.Bass("TRN2", target_bir_lowering=False)
    C = Ctx()
    C.nc, C.dbg, C.stack = nc, dbg, []

    def ent(cm):
        C.stack.append(cm)
        return cm.__enter__()

    C.ent = ent

    C.xb = nc.dram_tensor("xb", [S, D], F32, kind="ExternalInput").ap()
    C.wqkvT = nc.dram_tensor("wqkvT", [24, 128, 8, 128], F32, kind="ExternalInput").ap()
    C.bqkv = nc.dram_tensor("bqkv", [128, 24], F32, kind="ExternalInput").ap()
    C.wpreT = nc.dram_tensor("wpreT", [16, 16], F32, kind="ExternalInput").ap()
    C.wpostT = nc.dram_tensor("wpostT", [16, 16], F32, kind="ExternalInput").ap()
    C.woutT = nc.dram_tensor("woutT", [D, D], F32, kind="ExternalInput").ap()
    C.boutc = nc.dram_tensor("boutc", [128, 8], F32, kind="ExternalInput").ap()
    C.wfcT = nc.dram_tensor("wfcT", [D, 4 * D], F32, kind="ExternalInput").ap()
    C.lnwc = nc.dram_tensor("lnwc", [128, 32], F32, kind="ExternalInput").ap()
    C.lnbc = nc.dram_tensor("lnbc", [128, 32], F32, kind="ExternalInput").ap()
    C.wprojT = nc.dram_tensor("wprojT", [4 * D, D], F32, kind="ExternalInput").ap()
    C.identD = nc.dram_tensor("identD", [128, 128], BF16, kind="ExternalInput").ap()
    C.m8D = nc.dram_tensor("m8D", [128, 128], BF16, kind="ExternalInput").ap()
    C.e1D = nc.dram_tensor("e1D", [16, 128], BF16, kind="ExternalInput").ap()
    C.costabD = nc.dram_tensor("costabD", [128, S], BF16, kind="ExternalInput").ap()
    C.sintabD = nc.dram_tensor("sintabD", [128, S], BF16, kind="ExternalInput").ap()

    C.x1own = nc.dram_tensor("x1own", [OWN, D], F32, kind="ExternalOutput").ap()

    C.qstage = nc.dram_tensor("qstage", [D, OWN], BF16).ap()
    C.spre = nc.dram_tensor("spre", [NSB, 64, 512], BF16).ap()
    C.xnstage = nc.dram_tensor("xnstage", [128, 8, OWN], BF16).ap()
    C.wprojst = nc.dram_tensor("wprojst", [8, 128, 32, 128], BF16).ap()
    C.wfcst = nc.dram_tensor("wfcst", [32, 128, 8, 128], BF16).ap()
    C.qkvst = nc.dram_tensor("qkvst", [24, 128, S], BF16).ap()

    C.ident = ent(nc.sbuf_tensor("ident", [128, 128], BF16))
    C.m8 = ent(nc.sbuf_tensor("m8", [128, 128], BF16))
    C.e1 = ent(nc.sbuf_tensor("e1", [16, 128], BF16))
    C.onesc = ent(nc.sbuf_tensor("onesc", [128, 1], BF16))
    C.onescF = ent(nc.sbuf_tensor("onescF", [128, 1], F32))
    C.onesr = ent(nc.sbuf_tensor("onesr", [1, 512], BF16))
    C.onesrF = ent(nc.sbuf_tensor("onesrF", [1, 128], F32))
    C.epscol = ent(nc.sbuf_tensor("epscol", [128, 1], F32))
    C.zerocol = ent(nc.sbuf_tensor("zerocol", [128, 1], F32))
    C.ones128 = ent(nc.sbuf_tensor("ones128", [128, 128], BF16))
    C.pr = ent(nc.sbuf_tensor("pr", [128, 128], BF16))
    C.prabs = ent(nc.sbuf_tensor("prabs", [128, 128], BF16))
    C.pcm8 = ent(nc.sbuf_tensor("pcm8", [128, 128], BF16))
    C.bqs = ent(nc.sbuf_tensor("bqs", [128, 24], F32))
    C.m2col = ent(nc.sbuf_tensor("m2col", [128, 1], F32))
    C.c0 = ent(nc.sbuf_tensor("c0", [1, 1024], F32))
    C.rs1 = ent(nc.sbuf_tensor("rs1", [1, 1024], F32))
    C.c0cols = ent(nc.sbuf_tensor("c0cols", [128, 8], F32))
    C.lnwS = ent(nc.sbuf_tensor("lnwS", [128, 32], F32))
    C.lnbS = ent(nc.sbuf_tensor("lnbS", [128, 32], F32))
    C.lnbB = ent(nc.sbuf_tensor("lnbB", [128, 32], BF16))
    C.boutS = ent(nc.sbuf_tensor("boutS", [128, 8], F32))

    names = ["dA", "dB", "dC", "dD", "dE", "pe1", "pe2", "pe3", "ac1", "ac2",
             "ve1", "ve2", "ve3", "gp1", "q0", "q1", "w0", "w1", "l0", "l1",
             "m1", "m2", "m3", "m4", "m5", "m6", "m7", "m8s", "m9", "ma", "mb",
             "xA", "xB", "xC", "xD", "xE", "xF"]
    C.s = {k: Cnt(ent(nc.semaphore("s_" + k))) for k in names}

    ph_patterns(C)
    ph_quantize_fc(C)
    ph_quantize_proj(C)
    if dbg == "wq":
        C.d_wqfc = nc.dram_tensor("d_wqfc", [128, 8, 4 * D], BF16, kind="ExternalOutput").ap()
        C.d_c0 = nc.dram_tensor("d_c0", [1, 1024], F32, kind="ExternalOutput").ap()
        C.d_rs1 = nc.dram_tensor("d_rs1", [1, 1024], F32, kind="ExternalOutput").ap()
        with nc.Block() as blk:
            @blk.sync
            def _(sy):
                s = C.s
                s["dD"].inc(sy.dma_start(out=C.d_wqfc, in_=C.wfcst), 16)
                s["dD"].inc(sy.dma_start(out=C.d_c0, in_=C.c0[:]), 16)
                s["dD"].inc(sy.dma_start(out=C.d_rs1, in_=C.rs1[:]), 16)
                sy.wait_ge(s["dD"].h, s["dD"].n)
        finish(C)
        return nc, C
    ph_ln(C)
    if dbg == "xnT":
        finish(C)
        return nc, C
    ph_qkv(C)
    if dbg == "qkv":
        finish(C)
        return nc, C
    ph_attn(C)
    if dbg == "attn":
        finish(C)
        return nc, C
    ph_ffn(C)
    finish(C)
    return nc, C


def finish(C):
    if hasattr(C, "kvstack"):
        C.kvstack.close()
    if hasattr(C, "xnstack"):
        C.xnstack.close()
    for cm in reversed(C.stack):
        cm.__exit__(None, None, None)


def ph_patterns(C):
    nc, s = C.nc, C.s
    C.ppstack = ExitStack()
    C.ppat = C.ppstack.enter_context(nc.psum_tensor("ppat", [128, 128], F32))
    pp = C.ppat
    st = ExitStack()
    e = st.enter_context
    wps = e(nc.sbuf_tensor("wps", [16, 16], BF16))
    wpo = e(nc.sbuf_tensor("wpo", [16, 16], BF16))
    wpa = e(nc.sbuf_tensor("wpa", [16, 16], BF16))
    pscr = e(nc.sbuf_tensor("pscr", [128, 128], BF16))

    with nc.Block() as blk:
        @blk.sync
        def _(sy):
            for dst, src in ((C.ident, C.identD), (C.m8, C.m8D), (C.e1, C.e1D),
                             (C.bqs, C.bqkv), (C.lnwS, C.lnwc), (C.lnbS, C.lnbc),
                             (C.boutS, C.boutc)):
                s["dA"].inc(sy.dma_start(out=dst[:], in_=src), 16)

        @blk.gpsimd
        def _(g):
            s["dB"].inc(g.dma_start(out=wps[:], in_=C.wpreT), 16)
            s["dB"].inc(g.dma_start(out=wpo[:], in_=C.wpostT), 16)
            g.memset(C.onesc[:], 1.0)
            g.memset(C.onescF[:], 1.0)
            g.memset(C.onesrF[:], 1.0)
            g.memset(C.epscol[:], EPS)
            g.memset(C.zerocol[:], 0.0)
            g.memset(C.ones128[:, :], 1.0)
            s["gp1"].inc(g.memset(C.onesr[:], 1.0))

        @blk.scalar
        def _(a):
            a.wait_ge(s["dB"].h, 32)
            s["ac1"].inc(a.activation(wpa[:], wps[:], AF.Abs))
            a.wait_ge(s["dA"].h, 112)
            s["ac1"].inc(a.copy(C.lnbB[:], C.lnbS[:]))

        @blk.tensor
        def _(t):
            t.wait_ge(s["dA"].h, 112)
            t.wait_ge(s["dB"].h, 32)
            t.wait_ge(s["ac1"].h, 1)
            s["pe1"].inc(t.matmul(pp[:, 0:16], C.e1[:], wps[:], start=True, stop=True))
            s["pe1"].inc(t.matmul(pp[:, 16:32], C.e1[:], wpo[:], start=True, stop=True))
            s["pe1"].inc(t.matmul(pp[:, 32:48], C.e1[:], wpa[:], start=True, stop=True))

        @blk.vector
        def _(v):
            v.wait_ge(s["pe1"].h, 3)
            v.wait_ge(s["gp1"].h, 1)
            grid = pscr.rearrange("p (g e) -> p g e", e=8)
            for k, dst in ((0, C.pr), (1, C.pcm8), (2, C.prabs)):
                for i8 in range(8):
                    ins = v.tensor_copy(grid[:, :, i8], pp[:, 16 * k:16 * k + 16])
                    if i8 == 7:
                        s["ve1"].inc(ins)
                v.wait_ge(s["ve1"].h, s["ve1"].n)
                s["ve1"].inc(v.tensor_tensor(dst[:], pscr[:], C.m8[:], op=OP.mult))
                v.wait_ge(s["ve1"].h, s["ve1"].n)
            s["ve1"].inc(v.tensor_scalar(C.bqs[:, 0:8], C.bqs[:, 0:8], 0.125, None, op0=OP.mult))
    st.close()


def _emit_quant(C, tag, wT, n_tiles, width, post_tile, stage_fc=None):
    nc, s = C.nc, C.s
    st = ExitStack()
    e = st.enter_context
    scr = [e(nc.sbuf_tensor(f"q{tag}_s{k}", [128, width], F32)) for k in (0, 1)]
    msk = e(nc.sbuf_tensor(f"q{tag}_m", [128, width], F32))
    acc = e(nc.sbuf_tensor(f"q{tag}_a", [128, n_tiles], F32))
    gsc = e(nc.sbuf_tensor(f"q{tag}_g", [1, 2], F32))
    thr = e(nc.sbuf_tensor(f"q{tag}_t", [128, 2], F32))
    junk = e(nc.sbuf_tensor(f"q{tag}_j", [128, width], BF16))

    with nc.Block() as blk:
        bq = [s["q0"].n, s["q1"].n]
        bac = s["ac1"].n

        @blk.gpsimd
        def _(g):
            for i in range(n_tiles):
                if i >= 2:
                    g.wait_ge(s["ac1"].h, bac + i - 1)
                s["q%d" % (i % 2)].inc(g.dma_start(out=scr[i % 2][:], in_=wT[128 * i:128 * (i + 1), :]), 16)

        @blk.scalar
        def _(a):
            for i in range(n_tiles):
                a.wait_ge(s["q%d" % (i % 2)].h, bq[i % 2] + 16 * (i // 2 + 1))
                if i >= 1:
                    a.wait_ge(s["ac1"].h, bac + i)
                s["ac1"].inc(a.activation(junk[:], scr[i % 2][:], AF.Abs,
                                          accum_out=acc[:, i:i + 1]))

        @blk.tensor
        def _(t):
            t.wait_ge(s["ac1"].h, bac + n_tiles)
            s["pe1"].inc(t.matmul(C.ppat[0:1, 64:64 + n_tiles], C.onescF[:], acc[:],
                                  start=True, stop=True))

        @blk.vector
        def _(v):
            v.wait_ge(s["pe1"].h, s["pe1"].n)
            s["ve1"].inc(v.tensor_reduce(gsc[:, 0:1], C.ppat[0:1, 64:64 + n_tiles],
                                         axis=AX, op=OP.add))
            v.wait_ge(s["ve1"].h, s["ve1"].n)
            ntot = float(n_tiles * 128 * width)
            s["ve1"].inc(v.tensor_scalar(gsc[:, 0:1], gsc[:, 0:1], 0.5 / ntot, 0.5e-6,
                                         op0=OP.mult, op1=OP.add))
            v.wait_ge(s["ve1"].h, s["ve1"].n)
            s["ve1"].inc(v.tensor_scalar(gsc[:, 1:2], gsc[:, 0:1], -1.0, None, op0=OP.mult))

        @blk.tensor
        def _(t2):
            t2.wait_ge(s["ve1"].h, s["ve1"].n)
            s["pe1"].inc(t2.matmul(C.ppat[:, 96:98], C.onesrF[:], gsc[:],
                                   start=True, stop=True))

        @blk.vector
        def _(v2):
            v2.wait_ge(s["pe1"].h, s["pe1"].n)
            s["ve1"].inc(v2.tensor_copy(thr[:], C.ppat[:, 96:98]))
            v2.wait_ge(s["ve1"].h, s["ve1"].n)

    with nc.Block() as blk:
        bq = [s["q0"].n, s["q1"].n]
        bve2, bve3 = s["ve2"].n, s["ve3"].n
        per = 3

        @blk.gpsimd
        def _(g):
            for i in range(n_tiles):
                if i >= 2:
                    g.wait_ge(s["ve3"].h, bve3 + i - 1)
                s["q%d" % (i % 2)].inc(g.dma_start(out=scr[i % 2][:], in_=wT[128 * i:128 * (i + 1), :]), 16)

        @blk.vector
        def _(v):
            for i in range(n_tiles):
                src = scr[i % 2]
                v.wait_ge(s["q%d" % (i % 2)].h, bq[i % 2] + 16 * (i // 2 + 1))
                s["ve2"].inc(v.tensor_scalar(msk[:], src[:], thr[:, 0:1], None, op0=OP.is_ge))
                v.wait_ge(s["ve2"].h, s["ve2"].n)
                s["ve2"].inc(v.tensor_scalar(src[:], src[:], thr[:, 1:2], None, op0=OP.is_le))
                v.wait_ge(s["ve2"].h, s["ve2"].n)
                s["ve2"].inc(v.tensor_tensor(msk[:], msk[:], src[:], op=OP.subtract))
                v.wait_ge(s["ve2"].h, s["ve2"].n)
                post_tile(v, i, msk)

        if stage_fc is not None:
            @blk.sync
            def _(sy):
                for i in range(n_tiles):
                    sy.wait_ge(s["ve3"].h, bve3 + i + 1)
                    for f32 in range(32):
                        ins = sy.dma_start(out=C.wfcst[f32, :, i, :],
                                           in_=C._qfc_bufs[1][i % 2][:, 128 * f32:128 * (f32 + 1)])
                        s["w%d" % (i % 2)].inc(ins, 16)
                sy.wait_ge(s["w0"].h, s["w0"].n)
                sy.wait_ge(s["w1"].h, s["w1"].n)
    st.close()


def ph_quantize_fc(C):
    nc, s = C.nc, C.s
    stq = ExitStack()
    wqt = [stq.enter_context(nc.sbuf_tensor(f"qfc_o{k}", [128, 4 * D], BF16)) for k in (0, 1)]
    C._qfc_bufs = (stq, wqt)

    def post(v, i, wq):
        if i >= 2:
            v.wait_ge(s["w%d" % (i % 2)].h, C._qfc_bw[i % 2] + 16 * 32 * (i // 2))
        s["ve3"].inc(v.tensor_copy(wqt[i % 2][:], wq[:]))
        v.wait_ge(s["ve3"].h, s["ve3"].n)

    C._qfc_bw = [s["w0"].n, s["w1"].n]
    _emit_quant(C, "fc", C.wfcT, 8, 4 * D, post, C)
    stq.close()


def ph_quantize_proj(C):
    nc, s = C.nc, C.s
    n_tiles, width = 32, D
    st = ExitStack()
    e = st.enter_context
    scr = [e(nc.sbuf_tensor(f"qp_s{k}", [128, width], F32)) for k in (0, 1)]
    msk = e(nc.sbuf_tensor("qp_m", [128, width], F32))
    acc = e(nc.sbuf_tensor("qp_a", [128, n_tiles], F32))
    gsc = e(nc.sbuf_tensor("qp_g", [1, 2], F32))
    thr = e(nc.sbuf_tensor("qp_t", [128, 2], F32))
    junk = e(nc.sbuf_tensor("qp_j", [128, width], BF16))
    wqb = [e(nc.sbuf_tensor(f"qp_b{k}", [128, width], BF16)) for k in (0, 1)]
    wqs = [e(nc.sbuf_tensor(f"qp_w{k}", [128, width], BF16)) for k in (0, 1)]
    c0B = e(nc.sbuf_tensor("qp_c0b", [1, 1024], BF16))
    pc0 = [e(nc.psum_tensor(f"qp_pc{k}", [1, 512], F32)) for k in (0, 1)]
    prs = [e(nc.psum_tensor(f"qp_pr{k}", [1, 512], F32)) for k in (0, 1)]
    pcc = e(nc.psum_tensor("qp_pcc", [128, 8], F32))

    with nc.Block() as blk:
        bq = [s["q0"].n, s["q1"].n]
        bac = s["ac1"].n

        @blk.gpsimd
        def _(g):
            for i in range(n_tiles):
                if i >= 2:
                    g.wait_ge(s["ac1"].h, bac + i - 1)
                s["q%d" % (i % 2)].inc(g.dma_start(out=scr[i % 2][:], in_=C.wprojT[128 * i:128 * (i + 1), :]), 16)

        @blk.scalar
        def _(a):
            for i in range(n_tiles):
                a.wait_ge(s["q%d" % (i % 2)].h, bq[i % 2] + 16 * (i // 2 + 1))
                if i >= 1:
                    a.wait_ge(s["ac1"].h, bac + i)
                s["ac1"].inc(a.activation(junk[:], scr[i % 2][:], AF.Abs,
                                          accum_out=acc[:, i:i + 1]))

        @blk.tensor
        def _(t):
            t.wait_ge(s["ac1"].h, bac + n_tiles)
            s["pe1"].inc(t.matmul(C.ppat[0:1, 64:64 + n_tiles], C.onescF[:], acc[:],
                                  start=True, stop=True))

        @blk.vector
        def _(v):
            v.wait_ge(s["pe1"].h, s["pe1"].n)
            s["ve1"].inc(v.tensor_reduce(gsc[:, 0:1], C.ppat[0:1, 64:64 + n_tiles],
                                         axis=AX, op=OP.add))
            v.wait_ge(s["ve1"].h, s["ve1"].n)
            ntot = float(n_tiles * 128 * width)
            s["ve1"].inc(v.tensor_scalar(gsc[:, 0:1], gsc[:, 0:1], 0.5 / ntot, 0.5e-6,
                                         op0=OP.mult, op1=OP.add))
            v.wait_ge(s["ve1"].h, s["ve1"].n)
            s["ve1"].inc(v.tensor_scalar(gsc[:, 1:2], gsc[:, 0:1], -1.0, None, op0=OP.mult))

        @blk.tensor
        def _(t2):
            t2.wait_ge(s["ve1"].h, s["ve1"].n)
            s["pe1"].inc(t2.matmul(C.ppat[:, 96:98], C.onesrF[:], gsc[:],
                                   start=True, stop=True))

        @blk.vector
        def _(v2):
            v2.wait_ge(s["pe1"].h, s["pe1"].n)
            s["ve1"].inc(v2.tensor_copy(thr[:], C.ppat[:, 96:98]))
            v2.wait_ge(s["ve1"].h, s["ve1"].n)

    with nc.Block() as blk:
        bq = [s["q0"].n, s["q1"].n]
        bw = [s["w0"].n, s["w1"].n]
        bve2, bve3, bpe2 = s["ve2"].n, s["ve3"].n, s["pe2"].n
        VPT = 6

        @blk.gpsimd
        def _(g):
            for i in range(n_tiles):
                if i >= 2:
                    g.wait_ge(s["ve2"].h, bve2 + VPT * (i - 1))
                s["q%d" % (i % 2)].inc(g.dma_start(out=scr[i % 2][:], in_=C.wprojT[128 * i:128 * (i + 1), :]), 16)

        @blk.vector
        def _(v):
            for i in range(n_tiles):
                src = scr[i % 2]
                v.wait_ge(s["q%d" % (i % 2)].h, bq[i % 2] + 16 * (i // 2 + 1))
                if i >= 2:
                    v.wait_ge(s["pe2"].h, bpe2 + 4 * (i - 1))
                    v.wait_ge(s["w%d" % (i % 2)].h, bw[i % 2] + 16 * 8 * (i // 2))
                s["ve2"].inc(v.tensor_scalar(msk[:], src[:], thr[:, 0:1], None, op0=OP.is_ge))
                v.wait_ge(s["ve2"].h, s["ve2"].n)
                s["ve2"].inc(v.tensor_scalar(src[:], src[:], thr[:, 1:2], None, op0=OP.is_le))
                v.wait_ge(s["ve2"].h, s["ve2"].n)
                s["ve2"].inc(v.tensor_tensor(msk[:], msk[:], src[:], op=OP.subtract))
                v.wait_ge(s["ve2"].h, s["ve2"].n)
                s["ve2"].inc(v.tensor_copy(wqb[i % 2][:], msk[:]))
                v.wait_ge(s["ve2"].h, s["ve2"].n)
                s["ve2"].inc(v.tensor_scalar(msk[:], msk[:], C.lnwS[:, i:i + 1], None, op0=OP.mult))
                v.wait_ge(s["ve2"].h, s["ve2"].n)
                s["ve2"].inc(v.tensor_copy(wqs[i % 2][:], msk[:]))
                v.wait_ge(s["ve2"].h, s["ve2"].n)

        @blk.tensor
        def _(t):
            for i in range(n_tiles):
                t.wait_ge(s["ve2"].h, bve2 + VPT * (i + 1))
                fl = dict(start=(i == 0), stop=(i == n_tiles - 1))
                s["pe2"].inc(t.matmul(pc0[0][:], C.lnbB[:, i:i + 1], wqb[i % 2][:, 0:512], **fl))
                s["pe2"].inc(t.matmul(pc0[1][:], C.lnbB[:, i:i + 1], wqb[i % 2][:, 512:1024], **fl))
                s["pe2"].inc(t.matmul(prs[0][:], C.onesc[:], wqs[i % 2][:, 0:512], **fl))
                s["pe2"].inc(t.matmul(prs[1][:], C.onesc[:], wqs[i % 2][:, 512:1024], **fl))

        @blk.sync
        def _(sy):
            for i in range(n_tiles):
                sy.wait_ge(s["ve2"].h, bve2 + VPT * (i + 1))
                for o in range(8):
                    s["w%d" % (i % 2)].inc(sy.dma_start(out=C.wprojst[o, :, i, :],
                                                        in_=wqs[i % 2][:, 128 * o:128 * (o + 1)]), 16)
            sy.wait_ge(s["w0"].h, s["w0"].n)
            sy.wait_ge(s["w1"].h, s["w1"].n)

    with nc.Block() as blk:
        @blk.vector
        def _(v):
            s["ve1"].inc(v.tensor_copy(C.c0[:, 0:512], pc0[0][:]))
            s["ve1"].inc(v.tensor_copy(C.c0[:, 512:1024], pc0[1][:]))
            s["ve1"].inc(v.tensor_copy(C.rs1[:, 0:512], prs[0][:]))
            s["ve1"].inc(v.tensor_copy(C.rs1[:, 512:1024], prs[1][:]))
            v.wait_ge(s["ve1"].h, s["ve1"].n)
            s["ve1"].inc(v.tensor_copy(c0B[:], C.c0[:]))
            v.wait_ge(s["ve1"].h, s["ve1"].n)

        @blk.tensor
        def _(t):
            t.wait_ge(s["ve1"].h, s["ve1"].n)
            for o in range(8):
                s["pe1"].inc(t.matmul(pcc[:, o:o + 1], c0B[:, 128 * o:128 * (o + 1)],
                                      C.onesr[0:1, 0:1],
                                      start=True, stop=True))

        @blk.vector
        def _(v):
            v.wait_ge(s["pe1"].h, s["pe1"].n)
            s["ve1"].inc(v.tensor_copy(C.c0cols[:], pcc[:]))
            v.wait_ge(s["ve1"].h, s["ve1"].n)
    st.close()


def ph_ln(C):
    nc, s = C.nc, C.s
    C.xnstack = ExitStack()
    C.xnTo = C.xnstack.enter_context(nc.sbuf_tensor("xnTo", [128, 8, OWN], BF16))
    C.xnTx = C.xnstack.enter_context(nc.sbuf_tensor("xnTx", [128, 8, OWN], BF16))
    with ExitStack() as st:
        e = st.enter_context
        xt = [e(nc.sbuf_tensor(f"ln_x{k}", [128, D], F32)) for k in (0, 1)]
        xnb = [e(nc.sbuf_tensor(f"ln_n{k}", [128, D], BF16)) for k in (0, 1)]
        jk = e(nc.sbuf_tensor("ln_j", [128, D], F32))
        sc = e(nc.sbuf_tensor("ln_s", [128, 8], F32))
        ptr = e(nc.psum_tensor("ln_p", [128, 1024], F32))

        def ts_cnt(t):
            return 1 if t == 0 else 3 * t
        def ev_cnt(t):
            return 3 * t + 3

        def evac_dest(t):
            dstT = C.xnTo if t < 16 else C.xnTx
            tc = (128 * t) % OWN
            return dstT, tc

        with nc.Block() as blk:
            bl = [s["l0"].n, s["l1"].n]
            bac, bve1, bve2, bpe = s["ac1"].n, s["ve1"].n, s["ve2"].n, s["pe1"].n

            @blk.sync
            def _(sy):
                for t in range(32):
                    if t >= 2:
                        sy.wait_ge(s["ve2"].h, bve2 + ts_cnt(t - 2))
                        sy.wait_ge(s["ac1"].h, bac + 2 * (t - 2) + 1)
                    s["l%d" % (t % 2)].inc(sy.dma_start(out=xt[t % 2][:], in_=C.xb[128 * t:128 * (t + 1), :]), 16)

            @blk.scalar
            def _(a):
                for t in range(32):
                    a.wait_ge(s["l%d" % (t % 2)].h, bl[t % 2] + 16 * (t // 2 + 1))
                    if t >= 1:
                        a.wait_ge(s["ve1"].h, bve1 + 6 * (t - 1) + 3)
                        a.wait_ge(s["ac1"].h, bac + 2 * t)
                    s["ac1"].inc(a.activation(jk[:], xt[t % 2][:], AF.Square,
                                              accum_out=sc[:, 1:2]))
                    a.wait_ge(s["ve1"].h, bve1 + 6 * t + 5)
                    s["ac1"].inc(a.activation(sc[:, 4:5], sc[:, 3:4], AF.Sqrt, bias=C.epscol[:]))

            @blk.vector
            def _(v):
                for t in range(32):
                    v.wait_ge(s["l%d" % (t % 2)].h, bl[t % 2] + 16 * (t // 2 + 1))
                    s["ve1"].inc(v.reduce_sum(sc[:, 0:1], xt[t % 2][:], axis=AX))
                    v.wait_ge(s["ve1"].h, s["ve1"].n)
                    s["ve1"].inc(v.tensor_scalar(sc[:, 2:3], sc[:, 0:1], 1.0 / D, None, op0=OP.mult))
                    v.wait_ge(s["ve1"].h, s["ve1"].n)
                    v.wait_ge(s["ac1"].h, bac + 2 * t + 1)
                    s["ve1"].inc(v.tensor_scalar(sc[:, 3:4], sc[:, 1:2], 1.0 / D, None, op0=OP.mult))
                    v.wait_ge(s["ve1"].h, s["ve1"].n)
                    s["ve1"].inc(v.tensor_scalar(sc[:, 6:7], sc[:, 2:3], sc[:, 2:3], None, op0=OP.mult))
                    v.wait_ge(s["ve1"].h, s["ve1"].n)
                    s["ve1"].inc(v.tensor_tensor(sc[:, 3:4], sc[:, 3:4], sc[:, 6:7], op=OP.subtract))
                    v.wait_ge(s["ve1"].h, s["ve1"].n)
                    v.wait_ge(s["ac1"].h, bac + 2 * t + 2)
                    s["ve1"].inc(v.reciprocal(sc[:, 5:6], sc[:, 4:5]))
                    v.wait_ge(s["ve1"].h, s["ve1"].n)
                    s["ve2"].inc(v.tensor_scalar(xnb[t % 2][:], xt[t % 2][:], sc[:, 2:3],
                                                 sc[:, 5:6], op0=OP.subtract, op1=OP.mult))
                    v.wait_ge(s["ve2"].h, s["ve2"].n)
                    if t >= 1:
                        v.wait_ge(s["pe1"].h, bpe + t)
                        dstT, tc = evac_dest(t - 1)
                        s["ve2"].inc(v.tensor_copy(dstT[:, 0:4, tc:tc + 128], ptr[:, 0:512]))
                        s["ve2"].inc(v.tensor_copy(dstT[:, 4:8, tc:tc + 128], ptr[:, 512:1024]))
                        v.wait_ge(s["ve2"].h, s["ve2"].n)
                v.wait_ge(s["pe1"].h, bpe + 32)
                dstT, tc = evac_dest(31)
                s["ve2"].inc(v.tensor_copy(dstT[:, 0:4, tc:tc + 128], ptr[:, 0:512]))
                s["ve2"].inc(v.tensor_copy(dstT[:, 4:8, tc:tc + 128], ptr[:, 512:1024]))
                v.wait_ge(s["ve2"].h, s["ve2"].n)

            @blk.tensor
            def _(t_):
                for t in range(32):
                    t_.wait_ge(s["ve2"].h, bve2 + (1 if t == 0 else 3 * t + 1))
                    for o in range(8):
                        ins = t_.matmul(ptr[:, 128 * o:128 * (o + 1)],
                                        xnb[t % 2][:, 128 * o:128 * (o + 1)],
                                        C.ident[:], start=True, stop=True)
                        if o == 7:
                            s["pe1"].inc(ins)

        with nc.Block() as blk:
            @blk.sync
            def _(sy):
                s["dD"].inc(sy.dma_start(out=C.xnstage, in_=C.xnTo[:]), 16)
                sy.wait_ge(s["dD"].h, s["dD"].n)

    if C.dbg == "xnT":
        C.d_xnT = nc.dram_tensor("d_xnT", [2, 128, 8, OWN], BF16, kind="ExternalOutput").ap()
        with nc.Block() as blk:
            @blk.sync
            def _(sy):
                s["dD"].inc(sy.dma_start(out=C.d_xnT[0], in_=C.xnTo[:]), 16)
                s["dD"].inc(sy.dma_start(out=C.d_xnT[1], in_=C.xnTx[:]), 16)
                sy.wait_ge(s["dD"].h, s["dD"].n)


def ph_qkv(C):
    nc, s = C.nc, C.s

    with ExitStack() as st:
        e = st.enter_context
        wl = e(nc.sbuf_tensor("qk_wl", [128, 2, 8, 128], BF16))
        ob = [e(nc.sbuf_tensor(f"qk_ob{k}", [128, S], BF16)) for k in (0, 1)]
        pm = [e(nc.psum_tensor(f"qk_pm{k}", [128, 512], F32)) for k in (0, 1)]

        chunks = []
        for o in range(24):
            ntc = 4 if o < 8 else 8
            for tc in range(ntc):
                chunks.append((o, tc, ntc))
        o_c0 = {}
        for ci, (o, tc, ntc) in enumerate(chunks):
            if o not in o_c0:
                o_c0[o] = ci

        with nc.Block() as blk:
            bq = [s["q0"].n, s["q1"].n]
            bw = [s["w0"].n, s["w1"].n]
            bpe2, bac2 = s["pe2"].n, s["ac2"].n

            @blk.gpsimd
            def _(g):
                for o in range(24):
                    if o >= 2:
                        g.wait_ge(s["pe2"].h, bpe2 + o_c0[o - 1])
                    s["q%d" % (o % 2)].inc(g.dma_start(out=wl[:, o % 2, :, :], in_=C.wqkvT[o]), 16)

            @blk.tensor
            def _(t_):
                for ci, (o, tc, ntc) in enumerate(chunks):
                    t_.wait_ge(s["q%d" % (o % 2)].h, bq[o % 2] + 16 * (o // 2 + 1))
                    if ci >= 2:
                        t_.wait_ge(s["ac2"].h, bac2 + ci - 1)
                    xsrc = C.xnTo if tc < 4 else C.xnTx
                    for k in range(8):
                        ins = t_.matmul(pm[ci % 2][:], wl[:, o % 2, k, :],
                                        xsrc[:, k, (tc % 4) * 512:(tc % 4) * 512 + 512],
                                        start=(k == 0), stop=(k == 7))
                        if k == 7:
                            s["pe2"].inc(ins)

            @blk.scalar
            def _(a):
                for ci, (o, tc, ntc) in enumerate(chunks):
                    a.wait_ge(s["pe2"].h, bpe2 + ci + 1)
                    if tc == 0 and o >= 2:
                        a.wait_ge(s["w%d" % (o % 2)].h, bw[o % 2] + 16 * (o // 2))
                    s["ac2"].inc(a.activation(ob[o % 2][:, tc * 512:tc * 512 + 512],
                                              pm[ci % 2][:], AF.Identity,
                                              bias=C.bqs[:, o:o + 1],
                                              scale=0.125 if o < 8 else 1.0))

            @blk.sync
            def _(sy):
                for ci, (o, tc, ntc) in enumerate(chunks):
                    if tc == ntc - 1:
                        sy.wait_ge(s["ac2"].h, bac2 + ci + 1)
                        w = 2048 if o < 8 else S
                        s["w%d" % (o % 2)].inc(sy.dma_start(out=C.qkvst[o, :, 0:w],
                                                            in_=ob[o % 2][:, 0:w]), 16)
                sy.wait_ge(s["w0"].h, s["w0"].n)
                sy.wait_ge(s["w1"].h, s["w1"].n)

    C.xnstack.close()

    C.kvstack = ExitStack()
    C.blkK = C.kvstack.enter_context(nc.sbuf_tensor("blkK", [128, 8, S], BF16))
    C.vtok = C.kvstack.enter_context(nc.sbuf_tensor("vtok", [128, 32, D], BF16))
    with ExitStack() as st:
        e = st.enter_context
        ld = [e(nc.sbuf_tensor(f"qb_l{k}", [128, S], BF16)) for k in (0, 1)]
        qro = [e(nc.sbuf_tensor(f"qb_qr{k}", [128, 2, OWN], BF16)) for k in (0, 1)]
        rt1 = e(nc.sbuf_tensor("qb_r1", [128, S], BF16))
        rt2 = e(nc.sbuf_tensor("qb_r2", [128, S], BF16))
        cosS = e(nc.sbuf_tensor("qb_cos", [128, S], BF16))
        sinS = e(nc.sbuf_tensor("qb_sin", [128, S], BF16))
        pv = [e(nc.psum_tensor(f"qb_pv{k}", [128, 512], F32)) for k in (0, 1)]

        with nc.Block() as blk:
            bdA = s["dA"].n
            blq = [s["l0"].n, s["l1"].n]
            bw = [s["w0"].n, s["w1"].n]
            bpe3, bve1, bve2, bve3 = s["pe3"].n, s["ve1"].n, s["ve2"].n, s["ve3"].n

            @blk.sync
            def _(sy):
                s["dA"].inc(sy.dma_start(out=cosS[:], in_=C.costabD), 16)
                s["dA"].inc(sy.dma_start(out=sinS[:], in_=C.sintabD), 16)
                for i, o in enumerate(range(16, 24)):
                    if i >= 2:
                        sy.wait_ge(s["ve3"].h, bve3 + 8 * (i - 1))
                    s["l%d" % (i % 2)].inc(sy.dma_start(out=ld[i % 2][:], in_=C.qkvst[o]), 16)
                for i in range(4):
                    sy.wait_ge(s["ve3"].h, bve3 + 64 + i)
                    s["l0"].inc(sy.dma_start(out=ld[0][:], in_=C.qkvst[8 + i]), 16)
                    s["l1"].inc(sy.dma_start(out=ld[1][:], in_=C.qkvst[12 + i]), 16)
                for i in range(4):
                    if i == 0:
                        sy.wait_ge(s["ve3"].h, bve3 + 68)
                    else:
                        sy.wait_ge(s["ve2"].h, bve2 + i)
                    s["l0"].inc(sy.dma_start(out=ld[0][:, 0:OWN], in_=C.qkvst[i, :, 0:OWN]), 16)
                    s["l1"].inc(sy.dma_start(out=ld[1][:, 0:OWN], in_=C.qkvst[4 + i, :, 0:OWN]), 16)
                    sy.wait_ge(s["ve2"].h, bve2 + i + 1)
                    s["w%d" % (i % 2)].inc(sy.dma_start(out=C.qstage[128 * i:128 * (i + 1), :],
                                                        in_=qro[i % 2][:, 0, :]), 16)
                    s["w%d" % (i % 2)].inc(sy.dma_start(out=C.qstage[512 + 128 * i:512 + 128 * (i + 1), :],
                                                        in_=qro[i % 2][:, 1, :]), 16)
                sy.wait_ge(s["w0"].h, s["w0"].n)
                sy.wait_ge(s["w1"].h, s["w1"].n)

            @blk.tensor
            def _(t_):
                for i in range(8):
                    oc = i
                    t_.wait_ge(s["l%d" % (i % 2)].h, blq[i % 2] + 16 * (i // 2 + 1))
                    for q in range(8):
                        gq = i * 8 + q
                        if gq >= 2:
                            t_.wait_ge(s["ve3"].h, bve3 + gq - 1)
                        for jj in range(4):
                            jt = q * 4 + jj
                            ins = t_.matmul(pv[q % 2][:, 128 * jj:128 * (jj + 1)],
                                            ld[i % 2][:, 128 * jt:128 * (jt + 1)],
                                            C.ident[:], start=True, stop=True)
                            if jj == 3:
                                s["pe3"].inc(ins)

            @blk.vector
            def _(v):
                for i in range(8):
                    oc = i
                    for q in range(8):
                        gq = i * 8 + q
                        v.wait_ge(s["pe3"].h, bpe3 + gq + 1)
                        s["ve3"].inc(v.tensor_copy(
                            C.vtok[:, q * 4:(q + 1) * 4, 128 * oc:128 * (oc + 1)],
                            pv[q % 2][:]))
                        v.wait_ge(s["ve3"].h, s["ve3"].n)
                v.wait_ge(s["dA"].h, bdA + 32)
                for i in range(4):
                    v.wait_ge(s["l0"].h, blq[0] + 16 * (4 + i + 1))
                    v.wait_ge(s["l1"].h, blq[1] + 16 * (4 + i + 1))
                    for half in range(2):
                        op = OP.subtract if half == 0 else OP.add
                        ta, tb = (cosS, sinS) if half == 0 else (sinS, cosS)
                        s["ve1"].inc(v.tensor_tensor(rt1[:], ld[0][:], ta[:], op=OP.mult))
                        s["ve1"].inc(v.tensor_tensor(rt2[:], ld[1][:], tb[:], op=OP.mult))
                        v.wait_ge(s["ve1"].h, s["ve1"].n)
                        for hl in range(4):
                            h = 4 * i + hl
                            c, r = h // 2, h % 2
                            ins = v.tensor_tensor(
                                C.blkK[64 * half + 32 * r:64 * half + 32 * r + 32, c, :],
                                rt1[32 * hl:32 * hl + 32, :],
                                rt2[32 * hl:32 * hl + 32, :], op=op)
                            if half == 1 and hl == 3:
                                s["ve3"].inc(ins)
                            else:
                                s["ve1"].inc(ins)
                        v.wait_ge(s["ve1"].h, s["ve1"].n)
                    v.wait_ge(s["ve3"].h, s["ve3"].n)
                for i in range(4):
                    v.wait_ge(s["l0"].h, blq[0] + 16 * (8 + i + 1))
                    v.wait_ge(s["l1"].h, blq[1] + 16 * (8 + i + 1))
                    if i >= 2:
                        v.wait_ge(s["w%d" % (i % 2)].h, bw[i % 2] + 32 * (i // 2))
                    for half in range(2):
                        op = OP.subtract if half == 0 else OP.add
                        ta, tb = (cosS, sinS) if half == 0 else (sinS, cosS)
                        s["ve1"].inc(v.tensor_tensor(rt1[:, 0:OWN], ld[0][:, 0:OWN], ta[:, 0:OWN], op=OP.mult))
                        s["ve1"].inc(v.tensor_tensor(rt2[:, 0:OWN], ld[1][:, 0:OWN], tb[:, 0:OWN], op=OP.mult))
                        v.wait_ge(s["ve1"].h, s["ve1"].n)
                        for hl in range(4):
                            ins = v.tensor_tensor(qro[i % 2][32 * hl:32 * hl + 32, half, :],
                                                  rt1[32 * hl:32 * hl + 32, 0:OWN],
                                                  rt2[32 * hl:32 * hl + 32, 0:OWN], op=op)
                            if half == 1 and hl == 3:
                                s["ve2"].inc(ins)
                            else:
                                s["ve1"].inc(ins)
                        v.wait_ge(s["ve1"].h, s["ve1"].n)
                    v.wait_ge(s["ve2"].h, s["ve2"].n)

    if C.dbg == "qkv":
        C.d_blkK = nc.dram_tensor("d_blkK", [128, 8, S], BF16, kind="ExternalOutput").ap()
        C.d_vtok = nc.dram_tensor("d_vtok", [128, 32, D], BF16, kind="ExternalOutput").ap()
        C.d_q = nc.dram_tensor("d_q", [D, OWN], BF16, kind="ExternalOutput").ap()
        with nc.Block() as blk:
            @blk.sync
            def _(sy):
                s["dD"].inc(sy.dma_start(out=C.d_blkK, in_=C.blkK[:]), 16)
                s["dD"].inc(sy.dma_start(out=C.d_vtok, in_=C.vtok[:]), 16)
                s["dD"].inc(sy.dma_start(out=C.d_q, in_=C.qstage), 16)
                sy.wait_ge(s["dD"].h, s["dD"].n)


def ph_attn(C):
    nc, s = C.nc, C.s
    C.ppstack.close()
    with ExitStack() as st:
        e = st.enter_context
        sbb = e(nc.sbuf_tensor("at_sb", [128, 4, S], BF16))
        blkQ = [e(nc.sbuf_tensor(f"at_bq{k}", [128, 8, 512], BF16)) for k in (0, 1)]
        blkQs = e(nc.sbuf_tensor("at_bqs", [128, 8, 512], BF16))
        qsl = [e(nc.sbuf_tensor(f"at_q{k}", [128, 8, 128], BF16)) for k in (0, 1)]
        mneg = [e(nc.sbuf_tensor(f"at_mn{k}", [128, 16], F32)) for k in (0, 1)]
        zcols = [e(nc.sbuf_tensor(f"at_zc{k}", [128, 8], F32)) for k in (0, 1)]
        zrec = e(nc.sbuf_tensor("at_zr", [128, 2], F32))
        pat = [e(nc.sbuf_tensor(f"at_pt{k}", [128, 128], BF16)) for k in (0, 1)]
        lscr = e(nc.sbuf_tensor("at_ls", [128, 2, 512], BF16))
        nqt = e(nc.sbuf_tensor("at_nqt", [128, 16], F32))
        nqsB = e(nc.sbuf_tensor("at_nqs", [128, 16], BF16))
        ssb = [e(nc.sbuf_tensor(f"at_ss{k}", [64, 512], BF16)) for k in (0, 1)]
        ksq = e(nc.sbuf_tensor("at_ksq", [128, 512], BF16))
        bo2 = e(nc.sbuf_tensor("at_bo2", [128, 2], BF16))
        mrun = e(nc.sbuf_tensor("at_mr", [128, 4], F32))
        t12 = e(nc.sbuf_tensor("at_t12", [128, 1], F32))
        pl = [e(nc.psum_tensor(f"at_pl{k}", [128, 512], F32)) for k in (0, 1)]
        plm = [e(nc.psum_tensor(f"at_pm{k}", [128, 512], F32)) for k in (0, 1)]
        ptp = [e(nc.psum_tensor(f"at_pp{k}", [128, 512], F32)) for k in (0, 1)]
        pav = [e(nc.psum_tensor(f"at_pv{k}", [64, 32], F32)) for k in (0, 1)]

        mrB = st.enter_context(nc.sbuf_tensor("at_mrb", [2, 1], BF16))
        m2B = st.enter_context(nc.sbuf_tensor("at_m2b", [1, 1], BF16))
        with nc.Block() as blk:
            bac0, bpe0, bve0 = s["ac1"].n, s["pe1"].n, s["ve1"].n

            @blk.gpsimd
            def _(g):
                s["gp1"].inc(g.memset(bo2[:], 0.0))
                g.memset(t12[:], 1e-12)
                g.wait_ge(s["gp1"].h, s["gp1"].n)
                g.memset(bo2[0:32, 0:1], 1.0)
                g.memset(bo2[64:96, 0:1], 1.0)
                g.memset(bo2[32:64, 1:2], 1.0)
                s["gp1"].inc(g.memset(bo2[96:128, 1:2], 1.0))

            @blk.scalar
            def _(a):
                a.wait_ge(s["gp1"].h, s["gp1"].n)
                for it in range(64):
                    c, jc = it // 8, it % 8
                    if it >= 1:
                        a.wait_ge(s["pe1"].h, bpe0 + it)
                    s["ac1"].inc(a.activation(ksq[:], C.blkK[:, c, jc * 512:jc * 512 + 512],
                                              AF.Square))

            @blk.tensor
            def _(t_):
                for it in range(64):
                    t_.wait_ge(s["ac1"].h, bac0 + it + 1)
                    if it >= 1:
                        t_.wait_ge(s["ve1"].h, bve0 + 2 * it)
                    s["pe1"].inc(t_.matmul(pl[1][0:2, :], bo2[:], ksq[:],
                                           start=True, stop=True))

            @blk.vector
            def _(v):
                for it in range(64):
                    v.wait_ge(s["pe1"].h, bpe0 + it + 1)
                    s["ve1"].inc(v.reduce_max(mrun[0:2, 1:2], pl[1][0:2, :], axis=AX))
                    v.wait_ge(s["ve1"].h, s["ve1"].n)
                    if it == 0:
                        s["ve1"].inc(v.tensor_copy(mrun[0:2, 0:1], mrun[0:2, 1:2]))
                    else:
                        s["ve1"].inc(v.tensor_tensor(mrun[0:2, 0:1], mrun[0:2, 0:1],
                                                     mrun[0:2, 1:2], op=OP.max))
                    v.wait_ge(s["ve1"].h, s["ve1"].n)
                s["ve1"].inc(v.tensor_copy(mrB[:], mrun[0:2, 0:1]))
                v.wait_ge(s["ve1"].h, s["ve1"].n)

            @blk.tensor
            def _(t2):
                t2.wait_ge(s["ve1"].h, s["ve1"].n)
                s["pe2"].inc(t2.matmul(pl[0][0:1, 0:2], mrB[:], C.ident[0:2, 0:2],
                                       start=True, stop=True))

            @blk.vector
            def _(v2):
                v2.wait_ge(s["pe2"].h, s["pe2"].n)
                s["ve1"].inc(v2.reduce_max(mrun[0:1, 3:4], pl[0][0:1, 0:2], axis=AX))
                v2.wait_ge(s["ve1"].h, s["ve1"].n)
                s["ve1"].inc(v2.tensor_copy(m2B[:], mrun[0:1, 3:4]))
                v2.wait_ge(s["ve1"].h, s["ve1"].n)

            @blk.tensor
            def _(t3):
                t3.wait_ge(s["ve1"].h, s["ve1"].n)
                s["pe2"].inc(t3.matmul(pl[0][:, 4:5], C.onesr[0:1, 0:128], m2B[:],
                                       start=True, stop=True))

            @blk.vector
            def _(v3):
                v3.wait_ge(s["pe2"].h, s["pe2"].n)
                s["ve1"].inc(v3.tensor_copy(C.m2col[:], pl[0][:, 4:5]))
                v3.wait_ge(s["ve1"].h, s["ve1"].n)

        NB_, GPB_, SBPB = ATTN_NBAT, GPB, 4
        with nc.Block() as blk:
            bq = [s["q0"].n, s["q1"].n]
            bw = [s["w0"].n, s["w1"].n]
            bm = {k: s[k].n for k in ("m1", "m2", "m3", "m4", "m5", "m6", "m7")}
            bA1, bA2, bA3, bA4 = s["pe1"].n, s["ve1"].n, s["pe2"].n, s["ac1"].n
            bZ, bB1, bB2d, bB2a = s["ve2"].n, s["pe3"].n, s["ve3"].n, s["ac2"].n
            bZb = s["mb"].n
            bAV, bSE = s["m8s"].n, s["m9"].n

            def pe_qk(t_, grp, jc):
                bat, grpl = grp // GPB_, grp % GPB_
                n = grp * 8 + jc
                if n >= 2:
                    t_.wait_ge(s["ve1"].h, bA2 + n - 1)
                if jc == 0:
                    t_.wait_ge(s["m2"].h, bm["m2"] + bat + 1)
                for c in range(8):
                    ch = c // 2
                    ins = t_.matmul(pl[n % 2][32 * ch:32 * ch + 32, :],
                                    blkQ[bat % 2][:, c, grpl * 32:grpl * 32 + 32],
                                    C.blkK[:, c, jc * 512:jc * 512 + 512],
                                    start=(c % 2 == 0), stop=(c % 2 == 1),
                                    tile_position=(0, 32 * ch),
                                    skip_group_check=True)
                    if c == 7:
                        s["pe1"].inc(ins)

            def pe_pm(t_, grp, jc):
                bat, grpl = grp // GPB_, grp % GPB_
                n = grp * 8 + jc
                t_.wait_ge(s["ve1"].h, bA2 + n + 1)
                if n >= 2:
                    t_.wait_ge(s["ac1"].h, bA4 + n - 1)
                if grpl == 0 and jc == 0:
                    t_.wait_ge(s["m7"].h, bm["m7"] + bat + 1)
                s["pe2"].inc(t_.matmul(plm[n % 2][:], C.pr[:], lscr[:, n % 2, :],
                                       start=True, stop=True))

            def pe_A(t_, grp):
                pe_qk(t_, grp, 0)
                pe_qk(t_, grp, 1)
                for jc in range(2, 8):
                    pe_pm(t_, grp, jc - 2)
                    pe_qk(t_, grp, jc)
                pe_pm(t_, grp, 6)
                pe_pm(t_, grp, 7)

            def pe_B(t_, grp):
                t_.wait_ge(s["mb"].h, bZb + grp + 1)
                for jt in range(32):
                    jtq_g = grp * 8 + jt // 4
                    if jtq_g >= 2:
                        if jtq_g % 2 == 0:
                            t_.wait_ge(s["ve3"].h, bB2d + jtq_g // 2)
                        else:
                            t_.wait_ge(s["ac2"].h, bB2a + jtq_g // 2)
                    ins = t_.matmul(ptp[(jt // 4) % 2][:, (jt % 4) * 128:(jt % 4) * 128 + 128],
                                    sbb[:, grp % 4, jt * 128:jt * 128 + 128],
                                    pat[grp % 2][:], start=True, stop=True)
                    s["pe3"].inc(ins)

            def pe_AV(t_, sb):
                t_.wait_ge(s["ve3"].h, bB2d + 16 * (sb + 1))
                t_.wait_ge(s["ac2"].h, bB2a + 16 * (sb + 1))
                for gp in range(16):
                    if sb * 16 + gp >= 2:
                        t_.wait_ge(s["m9"].h, bSE + sb * 16 + gp - 1)
                    for jt in range(32):
                        ins = t_.matmul(pav[gp % 2][:],
                                        C.vtok[:, jt, gp * 64:gp * 64 + 64],
                                        sbb[:, 0:4, jt * 128 + gp * 8:jt * 128 + gp * 8 + 8],
                                        start=(jt == 0), stop=(jt == 31))
                    s["m8s"].inc(ins)

            @blk.gpsimd
            def _(g):
                for bat in range(NB_):
                    if bat >= 2:
                        g.wait_ge(s["m2"].h, bm["m2"] + bat - 1)
                    s["q%d" % (bat % 2)].inc(g.dma_start(
                        out=qsl[bat % 2][:, :, :],
                        in_=C.qstage[:, bat * 128:(bat + 1) * 128].rearrange(
                            "(a p) t -> p a t", p=128)), 16)

            @blk.tensor
            def _(t_):
                for bat in range(NB_):
                    t_.wait_ge(s["m3"].h, bm["m3"] + bat + 1)
                    t_.wait_ge(s["ve1"].h, bA2 + bat * 128)
                    if bat >= 1:
                        t_.wait_ge(s["m5"].h, bm["m5"] + bat)
                    for grpl in range(GPB_):
                        for c in range(8):
                            ch = c // 2
                            ins = t_.matmul(pl[0][32 * ch:32 * ch + 32, grpl:grpl + 1],
                                            blkQs[:, c, grpl * 32:grpl * 32 + 32],
                                            C.onesc[:], start=(c % 2 == 0), stop=(c % 2 == 1),
                                            tile_position=(0, 32 * ch),
                                            skip_group_check=True)
                    s["m4"].inc(ins)
                    t_.wait_ge(s["m5"].h, bm["m5"] + bat + 1)
                    t_.wait_ge(s["ac1"].h, bA4 + bat * 128)
                    s["m6"].inc(t_.matmul(plm[0][:, 0:16], C.prabs[:], nqsB[:],
                                          start=True, stop=True))
                    for k in range(SBPB):
                        sb = bat * SBPB + k
                        g0 = sb * 4
                        pe_A(t_, g0)
                        pe_A(t_, g0 + 1)
                        pe_B(t_, g0)
                        pe_A(t_, g0 + 2)
                        pe_B(t_, g0 + 1)
                        pe_A(t_, g0 + 3)
                        pe_B(t_, g0 + 2)
                        pe_B(t_, g0 + 3)
                        pe_AV(t_, sb)

            @blk.scalar
            def _(a):
                for bat in range(NB_):
                    a.wait_ge(s["m2"].h, bm["m2"] + bat + 1)
                    if bat >= 1:
                        a.wait_ge(s["m4"].h, bm["m4"] + bat)
                    s["m3"].inc(a.activation(blkQs[:, :, :], blkQ[bat % 2][:, :, :], AF.Square))
                    a.wait_ge(s["m4"].h, bm["m4"] + bat + 1)
                    if bat >= 1:
                        a.wait_ge(s["m6"].h, bm["m6"] + bat)
                    s["ma"].inc(a.activation(nqt[:], pl[0][:, 0:16], AF.Ln,
                                             scale=C.m2col[:], bias=t12[:]))
                    a.wait_ge(s["ma"].h, s["ma"].n)
                    s["m5"].inc(a.activation(nqsB[:], nqt[:], AF.Exp, scale=0.5))
                    for k in range(SBPB):
                        sb = bat * SBPB + k
                        g0 = sb * 4

                        def exps(grp):
                            bat_, grpl = grp // GPB_, grp % GPB_
                            a.wait_ge(s["m7"].h, bm["m7"] + bat_ + 1)
                            a.wait_ge(s["m8s"].h, bAV + (grp // 4) * 16)
                            if grp >= 2:
                                a.wait_ge(s["mb"].h, bZb + grp - 1)
                            for jc in range(8):
                                n = grp * 8 + jc
                                a.wait_ge(s["pe2"].h, bA3 + n + 1)
                                s["ac1"].inc(a.activation(
                                    sbb[:, grp % 4, jc * 512:jc * 512 + 512],
                                    plm[n % 2][:], AF.Exp,
                                    bias=mneg[bat_ % 2][:, grpl:grpl + 1],
                                    accum_out=zcols[grp % 2][:, jc:jc + 1]))

                        def oquads(grp):
                            for jtq in range(8):
                                jtq_g = grp * 8 + jtq
                                if jtq_g % 2 == 0:
                                    continue
                                a.wait_ge(s["pe3"].h, bB1 + grp * 32 + (jtq + 1) * 4)
                                s["ac2"].inc(a.activation(
                                    sbb[:, grp % 4, jtq * 512:jtq * 512 + 512],
                                    ptp[jtq % 2][:], AF.Identity, bias=C.zerocol[:]))

                        exps(g0)
                        exps(g0 + 1)
                        oquads(g0)
                        exps(g0 + 2)
                        oquads(g0 + 1)
                        exps(g0 + 3)
                        oquads(g0 + 2)
                        oquads(g0 + 3)

            @blk.vector
            def _(v):
                for bat in range(NB_):
                    v.wait_ge(s["q%d" % (bat % 2)].h, bq[bat % 2] + 16 * (bat // 2 + 1))
                    if bat >= 2:
                        v.wait_ge(s["pe1"].h, bA1 + (bat - 1) * 128)
                    s["ve2"].inc(v.memset(blkQ[bat % 2][:, :, :], 0.0))
                    v.wait_ge(s["ve2"].h, s["ve2"].n)
                    bqv = blkQ[bat % 2].rearrange("p c (g w) -> p c g w", w=32)
                    for c in range(8):
                        for hl in range(2):
                            h = 2 * c + hl
                            for half in range(2):
                                rows0 = half * 64 + hl * 32
                                a_idx = h // 4 + 4 * half
                                p0 = (32 * h) % 128
                                cb = (c % 2) * 16 + hl * 8
                                ins = v.tensor_copy(
                                    bqv[rows0:rows0 + 32, c, :, cb:cb + 8],
                                    qsl[bat % 2][p0:p0 + 32, a_idx, :].rearrange(
                                        "p (g w) -> p g w", w=8))
                                if c == 7 and hl == 1 and half == 1:
                                    s["m2"].inc(ins)
                                else:
                                    s["ve2"].inc(ins)
                    v.wait_ge(s["m2"].h, s["m2"].n)
                    v.wait_ge(s["m6"].h, bm["m6"] + bat + 1)
                    if bat >= 2:
                        v.wait_ge(s["ac1"].h, bA4 + (bat - 1) * 128)
                    s["m7"].inc(v.tensor_scalar(mneg[bat % 2][:], plm[0][:, 0:16],
                                                -1.0, None, op0=OP.mult))
                    v.wait_ge(s["m7"].h, s["m7"].n)
                    for k in range(SBPB):
                        sb = bat * SBPB + k
                        g0 = sb * 4

                        def evacs(grp):
                            for jc in range(8):
                                n = grp * 8 + jc
                                v.wait_ge(s["pe1"].h, bA1 + n + 1)
                                if n >= 2:
                                    v.wait_ge(s["pe2"].h, bA3 + n - 1)
                                s["ve1"].inc(v.tensor_copy(lscr[:, n % 2, :], pl[n % 2][:]))

                        def zchain(grp):
                            v.wait_ge(s["ac1"].h, bA4 + grp * 8 + 8)
                            if grp >= 2:
                                v.wait_ge(s["pe3"].h, bB1 + (grp - 1) * 32)
                            s["ve2"].inc(v.reduce_sum(zrec[:, 0:1], zcols[grp % 2][:], axis=AX))
                            v.wait_ge(s["ve2"].h, s["ve2"].n)
                            s["ve2"].inc(v.reciprocal(zrec[:, 1:2], zrec[:, 0:1]))
                            v.wait_ge(s["ve2"].h, s["ve2"].n)
                            s["mb"].inc(v.tensor_scalar(pat[grp % 2][:], C.pcm8[:],
                                                        zrec[:, 1:2], None, op0=OP.mult))

                        def equads(grp):
                            for jtq in range(0, 8, 2):
                                v.wait_ge(s["pe3"].h, bB1 + grp * 32 + (jtq + 1) * 4)
                                s["ve3"].inc(v.tensor_copy(
                                    sbb[:, grp % 4, jtq * 512:jtq * 512 + 512],
                                    ptp[jtq % 2][:]))

                        def sevacs(sb_):
                            for gp in range(16):
                                v.wait_ge(s["m8s"].h, bAV + sb_ * 16 + gp + 1)
                                if sb_ >= 2:
                                    v.wait_ge(s["w%d" % (sb_ % 2)].h,
                                              bw[sb_ % 2] + 16 * (sb_ // 2))
                                s["m9"].inc(v.tensor_copy(
                                    ssb[sb_ % 2][0:64, gp * 32:gp * 32 + 32], pav[gp % 2][:]))

                        evacs(g0)
                        zchain(g0)
                        evacs(g0 + 1)
                        zchain(g0 + 1)
                        equads(g0)
                        evacs(g0 + 2)
                        zchain(g0 + 2)
                        equads(g0 + 1)
                        evacs(g0 + 3)
                        zchain(g0 + 3)
                        equads(g0 + 2)
                        equads(g0 + 3)
                        sevacs(sb)

            @blk.sync
            def _(sy):
                for sb in range(NB_ * SBPB):
                    sy.wait_ge(s["m9"].h, bSE + (sb + 1) * 16)
                    s["w%d" % (sb % 2)].inc(sy.dma_start(out=C.spre[sb], in_=ssb[sb % 2][:]), 16)
                sy.wait_ge(s["w0"].h, s["w0"].n)
                sy.wait_ge(s["w1"].h, s["w1"].n)

    if C.dbg == "attn":
        C.d_spre = nc.dram_tensor("d_spre", [NSB, 64, 512], BF16, kind="ExternalOutput").ap()
        with nc.Block() as blk:
            @blk.sync
            def _(sy):
                s["dD"].inc(sy.dma_start(out=C.d_spre, in_=C.spre), 16)
                sy.wait_ge(s["dD"].h, s["dD"].n)


def ph_ffn(C):
    nc, s = C.nc, C.s
    C.kvstack.close()
    H4 = 4 * D
    with ExitStack() as st:
        e = st.enter_context
        wqfb = [e(nc.sbuf_tensor(f"ff_wqf{k}", [128, 8, 128], BF16)) for k in (0, 1)]
        wout = e(nc.sbuf_tensor("ff_wo", [64, 16, D], BF16))
        xnc = [e(nc.sbuf_tensor(f"ff_xn{k}", [128, 8, 512], BF16)) for k in (0, 1)]
        wqp = [e(nc.sbuf_tensor(f"ff_wp{k}", [128, 32, 128], BF16)) for k in (0, 1)]
        ssp = [e(nc.sbuf_tensor(f"ff_sp{k}", [64, 512], BF16)) for k in (0, 1)]
        sch = e(nc.sbuf_tensor("ff_s", [128, 8, 512], BF16))
        hbuf = e(nc.sbuf_tensor("ff_h", [128, 32, 512], BF16))
        hsq = e(nc.sbuf_tensor("ff_hsq", [128, 512], BF16))
        gch = e(nc.sbuf_tensor("ff_g", [128, 8, 512], BF16))
        g0 = e(nc.sbuf_tensor("ff_g0", [128, 512], BF16))
        rows = e(nc.sbuf_tensor("ff_r", [1, 8, 512], F32))
        rowsB = e(nc.sbuf_tensor("ff_rb", [1, 2, 512], BF16))
        rs1B = e(nc.sbuf_tensor("ff_rs1b", [1, D], BF16))
        xt = [e(nc.sbuf_tensor(f"ff_x{k}", [128, D], F32)) for k in (0, 1)]
        x1t = xt
        pfc = [e(nc.psum_tensor(f"ff_pf{k}", [128, 512], F32)) for k in (0, 1)]
        pso = [e(nc.psum_tensor(f"ff_po{k}", [128, 256], F32)) for k in (0, 1)]
        psA = e(nc.psum_tensor("ff_pa", [128, 512], F32))
        psB = e(nc.psum_tensor("ff_pb", [128, 512], F32))
        pbc = e(nc.psum_tensor("ff_pc", [128, 512], F32))

        with nc.Block() as blk:
            @blk.sync
            def _(sy):
                pass

            @blk.gpsimd
            def _(g):
                s["dB"].inc(g.dma_start(out=wout[:, :, :],
                                        in_=C.woutT.rearrange("(kk p) f -> p kk f", p=64)), 16)

            @blk.vector
            def _(v):
                s["ve1"].inc(v.tensor_copy(rs1B[:], C.rs1[:]))
                v.wait_ge(s["ve1"].h, s["ve1"].n)

        for tc in range(FFN_TCN):
          with nc.Block() as blk:
            bq = [s["q0"].n, s["q1"].n]
            bw = [s["w0"].n, s["w1"].n]
            bl = [s["l0"].n, s["l1"].n]
            bdB = s["dB"].n
            bpe1, bpe2, bpe3 = s["pe1"].n, s["pe2"].n, s["pe3"].n
            bac1, bac2 = s["ac1"].n, s["ac2"].n
            bve1, bve2, bve3 = s["ve1"].n, s["ve2"].n, s["ve3"].n
            bm1, bm2, bm3, bm8, bma, bmb = (s["m1"].n, s["m2"].n, s["m3"].n,
                                            s["m8s"].n, s["ma"].n, s["mb"].n)
            bxc = [s["xC"].n, s["xD"].n]
            bxab = s[("xA", "xB")[tc % 2]].n
            bxef = [s["xE"].n, s["xF"].n]

            @blk.sync
            def _(sy):
                s[("xA", "xB")[tc % 2]].inc(sy.dma_start(
                    out=xnc[tc % 2][:, :, :],
                    in_=C.xnstage[:, :, tc * 512:tc * 512 + 512]), 16)
                for sbl in range(16):
                    sb = tc * 16 + sbl
                    par = sbl % 2
                    if sbl >= 2:
                        sy.wait_ge(s["pe1"].h, bpe1 + 8 * (sbl - 1))
                    s["w%d" % par].inc(sy.dma_start(out=ssp[par][0:64, :], in_=C.spre[sb]), 16)
                if FFN_MODE >= 3:
                    for step in range(6):
                        if step >= 2:
                            tjs = step - 2
                            sy.wait_ge(s["ve3"].h, bve3 + 2 * tjs + 2)
                            r0 = tc * 512 + tjs * 128
                            s[("xC", "xD")[tjs % 2]].inc(sy.dma_start(
                                out=C.x1own[r0:r0 + 128, :], in_=xt[tjs % 2][:]), 16)
                        if step < 4:
                            if step >= 2:
                                sy.wait_ge(s[("xC", "xD")[step % 2]].h,
                                           bxc[step % 2] + 16 * (step // 2))
                            r0 = tc * 512 + step * 128
                            s["l%d" % (step % 2)].inc(sy.dma_start(
                                out=xt[step % 2][:], in_=C.xb[r0:r0 + 128, :]), 16)
                    sy.wait_ge(s["xC"].h, s["xC"].n)
                    sy.wait_ge(s["xD"].h, s["xD"].n)

            @blk.gpsimd
            def _(g):
                if FFN_MODE >= 1:
                    for f in range(32):
                        if f >= 2:
                            g.wait_ge(s["pe2"].h, bpe2 + f - 1)
                        s[("xE", "xF")[f % 2]].inc(g.dma_start(
                            out=wqfb[f % 2][:, :, :], in_=C.wfcst[f]), 16)
                if FFN_MODE >= 2:
                    for o in range(8):
                        if o >= 2:
                            g.wait_ge(s["pe3"].h, bpe3 + o - 1)
                        s["q%d" % (o % 2)].inc(g.dma_start(
                            out=wqp[o % 2][:, :, :], in_=C.wprojst[o]), 16)

            @blk.tensor
            def _(t_):
                t_.wait_ge(s["dB"].h, bdB)
                for sbl in range(16):
                    par = sbl % 2
                    t_.wait_ge(s["w%d" % par].h, bw[par] + 16 * (sbl // 2 + 1))
                    if sbl >= 2:
                        t_.wait_ge(s["ac1"].h, bac1 + 8 * (sbl - 1))
                    for o in range(8):
                        for k in range(16):
                            ins = t_.matmul(pso[par][:, o * 32:o * 32 + 32],
                                            wout[0:64, k, o * 128:o * 128 + 128],
                                            ssp[par][0:64, k * 32:k * 32 + 32],
                                            start=(k == 0), stop=(k == 15),
                                            skip_group_check=True)
                        s["pe1"].inc(ins)
                if FFN_MODE >= 1:
                    t_.wait_ge(s[("xA", "xB")[tc % 2]].h, bxab + 16)
                    for f in range(32):
                        if f >= 2:
                            t_.wait_ge(s["ac2"].h, bac2 + 2 * (f - 1))
                        t_.wait_ge(s[("xE", "xF")[f % 2]].h, bxef[f % 2] + 16 * (f // 2 + 1))
                        for k in range(8):
                            ins = t_.matmul(pfc[f % 2][:], wqfb[f % 2][:, k, :],
                                            xnc[tc % 2][:, k, :],
                                            start=(k == 0), stop=(k == 7))
                            if k == 7:
                                s["pe2"].inc(ins)
                        t_.wait_ge(s["ac2"].h, bac2 + 2 * f + 2)
                        t_.matmul(psA[:], C.ones128[:], hbuf[:, f, :],
                                  start=(f == 0), stop=(f == 31), skip_group_check=True)
                        ins = t_.matmul(psB[:], C.ones128[:], hsq[:],
                                        start=(f == 0), stop=(f == 31), skip_group_check=True)
                        s["m3"].inc(ins)
                if FFN_MODE >= 2:
                    t_.wait_ge(s["ve1"].h, bve1 + 8)
                    s["ma"].inc(t_.matmul(pbc[:], C.onesr[0:1, 0:128], rowsB[:, 1, :],
                                          start=True, stop=True))
                    for o in range(8):
                        t_.wait_ge(s["q%d" % (o % 2)].h, bq[o % 2] + 16 * (o // 2 + 1))
                        if o < 2:
                            t_.wait_ge(s["ac2"].h, bac2 + 64)
                        else:
                            t_.wait_ge(s["ve2"].h, bve2 + 2 * (o - 1))
                        t_.wait_ge(s["ve1"].h, bve1 + 3)
                        for k in range(32):
                            t_.matmul(pfc[o % 2][:], wqp[o % 2][:, k, :], hbuf[:, k, :],
                                      start=(k == 0), stop=False)
                        ins = t_.matmul(pfc[o % 2][:], rs1B[0:1, o * 128:o * 128 + 128],
                                        rowsB[:, 0, :], start=False, stop=True)
                        s["pe3"].inc(ins)
                if FFN_MODE >= 3:
                    for tjt in range(4):
                        t_.wait_ge(s["mb"].h, bmb + 8)
                        if tjt >= 1:
                            t_.wait_ge(s["ve3"].h, bve3 + 2 * tjt)
                        for o in range(8):
                            dstp = pfc[0] if o < 4 else pfc[1]
                            ins = t_.matmul(dstp[:, (o % 4) * 128:(o % 4) * 128 + 128],
                                            sch[:, o, tjt * 128:tjt * 128 + 128],
                                            C.ident[:], start=True, stop=True)
                            s["m8s"].inc(ins)

            @blk.scalar
            def _(a):
                for sbl in range(16):
                    par = sbl % 2
                    a.wait_ge(s["pe1"].h, bpe1 + 8 * sbl + 8)
                    for o in range(8):
                        s["ac1"].inc(a.activation(sch[:, o, sbl * 32:sbl * 32 + 32],
                                                  pso[par][:, o * 32:o * 32 + 32],
                                                  AF.Identity, bias=C.boutS[:, o:o + 1]))
                if FFN_MODE >= 1:
                    for f in range(32):
                        a.wait_ge(s["pe2"].h, bpe2 + f + 1)
                        if f >= 1:
                            a.wait_ge(s["m3"].h, bm3 + f)
                        s["ac2"].inc(a.activation(hbuf[:, f, :], pfc[f % 2][:], AF.Gelu,
                                                  bias=C.zerocol[:]))
                        a.wait_ge(s["ac2"].h, s["ac2"].n)
                        s["ac2"].inc(a.activation(hsq[:], hbuf[:, f, :], AF.Square))
                        a.wait_ge(s["ac2"].h, s["ac2"].n)
                if FFN_MODE >= 2:
                    a.wait_ge(s["ve1"].h, bve1 + 6 if False else bve1 + 6)
                    s["m1"].inc(a.activation(rows[:, 3, :], rows[:, 2, :], AF.Sqrt,
                                             bias=C.epscol[0:1, :]))
                    for o in range(8):
                        a.wait_ge(s["ve2"].h, bve2 + 2 * o + 2)
                        s["m2"].inc(a.activation(gch[:, o, :], g0[:], AF.Silu,
                                                 bias=C.c0cols[:, o:o + 1]))
                        a.wait_ge(s["m2"].h, s["m2"].n)

            @blk.vector
            def _(v):
                if FFN_MODE >= 1:
                    v.wait_ge(s["m3"].h, bm3 + 32)
                    s["ve1"].inc(v.tensor_scalar(rows[:, 0, :], psA[0:1, :], 1.0 / H4, None, op0=OP.mult))
                    v.wait_ge(s["ve1"].h, s["ve1"].n)
                    s["ve1"].inc(v.tensor_scalar(rows[:, 4, :], rows[:, 0, :], -1.0, None, op0=OP.mult))
                    v.wait_ge(s["ve1"].h, s["ve1"].n)
                    s["ve1"].inc(v.tensor_copy(rowsB[:, 0, :], rows[:, 4, :]))
                    v.wait_ge(s["ve1"].h, s["ve1"].n)
                    s["ve1"].inc(v.tensor_scalar(rows[:, 2, :], psB[0:1, :], 1.0 / H4, None, op0=OP.mult))
                    v.wait_ge(s["ve1"].h, s["ve1"].n)
                    s["ve1"].inc(v.tensor_tensor(rows[:, 5, :], rows[:, 0, :], rows[:, 0, :], op=OP.mult))
                    v.wait_ge(s["ve1"].h, s["ve1"].n)
                    s["ve1"].inc(v.tensor_tensor(rows[:, 2, :], rows[:, 2, :], rows[:, 5, :], op=OP.subtract))
                    v.wait_ge(s["ve1"].h, s["ve1"].n)
                if FFN_MODE >= 2:
                    v.wait_ge(s["m1"].h, bm1 + 1)
                    s["ve1"].inc(v.reciprocal(rows[:, 6, :], rows[:, 3, :]))
                    v.wait_ge(s["ve1"].h, s["ve1"].n)
                    s["ve1"].inc(v.tensor_copy(rowsB[:, 1, :], rows[:, 6, :]))
                    v.wait_ge(s["ve1"].h, s["ve1"].n)
                    v.wait_ge(s["ac1"].h, bac1 + 128)
                    for o in range(8):
                        v.wait_ge(s["pe3"].h, bpe3 + o + 1)
                        v.wait_ge(s["ma"].h, bma + 1)
                        s["ve2"].inc(v.tensor_copy(g0[:], pfc[o % 2][:]))
                        v.wait_ge(s["ve2"].h, s["ve2"].n)
                        s["ve2"].inc(v.tensor_tensor(g0[:], g0[:], pbc[:], op=OP.mult))
                        v.wait_ge(s["ve2"].h, s["ve2"].n)
                        v.wait_ge(s["m2"].h, bm2 + o + 1)
                        s["mb"].inc(v.tensor_tensor(sch[:, o, :], sch[:, o, :], gch[:, o, :],
                                                    op=OP.mult))
                        v.wait_ge(s["mb"].h, s["mb"].n)
                if FFN_MODE >= 3:
                    for tjt in range(4):
                        v.wait_ge(s["m8s"].h, bm8 + 8 * (tjt + 1))
                        v.wait_ge(s["l%d" % (tjt % 2)].h, bl[tjt % 2] + 16 * (tjt // 2 + 1))
                        s["ve3"].inc(v.tensor_tensor(xt[tjt % 2][:, 0:512], xt[tjt % 2][:, 0:512],
                                                     pfc[0][:], op=OP.add))
                        v.wait_ge(s["ve3"].h, s["ve3"].n)
                        s["ve3"].inc(v.tensor_tensor(xt[tjt % 2][:, 512:1024], xt[tjt % 2][:, 512:1024],
                                                     pfc[1][:], op=OP.add))
                        v.wait_ge(s["ve3"].h, s["ve3"].n)



EXT1 = [8, 16, 24, 32]
FULL1 = [0, 8, 16, 24]
CUME1 = [0, 8, 24, 48]
L1_NH = 16


def build_l1a():
    nc = bass.Bass("TRN2", target_bir_lowering=False)
    C = Ctx()
    C.nc, C.stack = nc, []

    def ent(cm):
        C.stack.append(cm)
        return cm.__enter__()

    C.ent = ent

    C.x1f = nc.dram_tensor("x1f", [S, D], F32, kind="ExternalInput").ap()
    C.x1q = nc.dram_tensor("x1q", [OWN, D], F32, kind="ExternalInput").ap()
    C.wqkv1t = nc.dram_tensor("wqkv1t", [24, 128, 8, 128], F32, kind="ExternalInput").ap()
    C.gateT = nc.dram_tensor("gateT", [D, 8], F32, kind="ExternalInput").ap()
    C.posr = nc.dram_tensor("posr", [4, 512], F32, kind="ExternalInput").ap()
    C.identD = nc.dram_tensor("identD", [128, 128], BF16, kind="ExternalInput").ap()

    C.s1ch = nc.dram_tensor("s1ch", [128, 8, OWN], BF16, kind="ExternalOutput").ap()
    C.xn1full = nc.dram_tensor("xn1full", [S, D], BF16, kind="ExternalOutput").ap()
    C.afful = nc.dram_tensor("afful", [S, 8], F32, kind="ExternalOutput").ap()
    C.zst = nc.dram_tensor("zst", [16, 4, 512], F32, kind="ExternalOutput").ap()
    C.qstg = nc.dram_tensor("qstg", [128, 8, OWN], BF16).ap()
    C.kst = nc.dram_tensor("kst", [128, 8, S], BF16).ap()

    C.ident = ent(nc.sbuf_tensor("ident", [128, 128], BF16))
    C.identF = ent(nc.sbuf_tensor("identF", [128, 128], F32))
    C.epscol = ent(nc.sbuf_tensor("epscol", [128, 1], F32))
    C.zeroc = ent(nc.sbuf_tensor("zeroc", [128, 1], F32))
    C.qscl = ent(nc.sbuf_tensor("qscl", [128, 1], F32))
    C.mnegc = ent(nc.sbuf_tensor("mnegc", [128, 1], F32))
    C.jcol0 = ent(nc.sbuf_tensor("jcol0", [128, 1], F32))
    C.jcolv = ent(nc.sbuf_tensor("jcolv", [128, 32], F32))
    C.onesF128 = ent(nc.sbuf_tensor("onesF128", [128, 1], F32))
    C.onesFr = ent(nc.sbuf_tensor("onesFr", [1, 128], F32))
    C.bones = ent(nc.sbuf_tensor("bones", [128, 2], BF16))
    C.gwF = ent(nc.sbuf_tensor("gwF", [128, 8, 8], F32))
    C.gwsB = ent(nc.sbuf_tensor("gwsB", [128, 8], F32))

    names = ["dA", "dB", "pe1", "pe2", "pe3", "ac1", "ac2", "ve1", "ve2",
             "ve3", "gp1", "q0", "q1", "w0", "w1", "l0", "l1", "m1", "m2",
             "m3", "m4", "m5", "m6"]
    C.s = {k: Cnt(ent(nc.semaphore("s_" + k))) for k in names}
    s = C.s

    with ExitStack() as st:
        pgs = st.enter_context(nc.psum_tensor("c_pgs", [1, 8], F32))
        pgb = st.enter_context(nc.psum_tensor("c_pgb", [128, 8], F32))
        with nc.Block() as blk:
            bdA, bgp1, bpe1, bve1 = s["dA"].n, s["gp1"].n, s["pe1"].n, s["ve1"].n

            @blk.sync
            def _(sy):
                s["dA"].inc(sy.dma_start(out=C.ident[:], in_=C.identD), 16)
                s["dA"].inc(sy.dma_start(
                    out=C.gwF[:, :, :],
                    in_=C.gateT.rearrange("(k p) e -> p k e", p=128)), 16)

            @blk.gpsimd
            def _(g):
                s["gp1"].inc(g.memset(C.epscol[:], EPS))
                s["gp1"].inc(g.memset(C.zeroc[:], 0.0))
                s["gp1"].inc(g.memset(C.qscl[:], 0.125))
                s["gp1"].inc(g.memset(C.onesF128[:], 1.0))
                s["gp1"].inc(g.memset(C.onesFr[:], 1.0))
                s["gp1"].inc(g.memset(C.bones[:, :], 0.0))
                g.wait_ge(s["gp1"].h, s["gp1"].n)
                s["gp1"].inc(g.memset(C.bones[0:64, 0:1], 1.0))
                g.wait_ge(s["gp1"].h, s["gp1"].n)
                s["gp1"].inc(g.memset(C.bones[64:128, 1:2], 1.0))
                s["gp1"].inc(g.iota(C.jcol0[:], [[0, 1]], base=0, channel_multiplier=1,
                                    allow_small_or_imprecise_dtypes=True))
                for jt in range(32):
                    s["gp1"].inc(g.iota(C.jcolv[:, jt:jt + 1], [[0, 1]], base=128 * jt,
                                        channel_multiplier=1,
                                        allow_small_or_imprecise_dtypes=True))

            @blk.vector
            def _(v):
                v.wait_ge(s["dA"].h, bdA + 32)
                s["ve1"].inc(v.tensor_copy(C.identF[:], C.ident[:]))
                v.wait_ge(s["pe1"].h, bpe1 + 1)
                s["ve1"].inc(v.tensor_copy(C.gwsB[0:1, 0:8], pgs[:]))
                v.wait_ge(s["pe1"].h, bpe1 + 2)
                s["ve1"].inc(v.tensor_copy(C.gwsB[:, :], pgb[:]))

            @blk.tensor
            def _(t_):
                t_.wait_ge(s["dA"].h, bdA + 32)
                t_.wait_ge(s["gp1"].h, bgp1 + 41)
                for k in range(8):
                    ins = t_.matmul(pgs[:], C.onesF128[:], C.gwF[:, k, :],
                                    start=(k == 0), stop=(k == 7))
                s["pe1"].inc(ins)
                t_.wait_ge(s["ve1"].h, bve1 + 2)
                s["pe1"].inc(t_.matmul(pgb[:], C.onesFr[:], C.gwsB[0:1, 0:8],
                                       start=True, stop=True))

    C.mxk = ent(nc.sbuf_tensor("mxk", [2, 64], F32))
    C.mxq = ent(nc.sbuf_tensor("mxq", [2, 32], F32))
    C.vtok1 = ent(nc.sbuf_tensor("vtok1", [128, 32, 16 * 65], BF16))

    C.lnstack = ExitStack()
    xn1T = C.lnstack.enter_context(nc.sbuf_tensor("xn1T", [128, 8, S], BF16))
    xnqs = ExitStack()
    xnqT = xnqs.enter_context(nc.sbuf_tensor("xnqT", [128, 8, OWN], BF16))
    afls = ExitStack()
    afl = afls.enter_context(nc.sbuf_tensor("afl", [128, 32, 8], F32))

    def emit_ln(xdram, ntile, dstT, out_tm, gate=False, tag=""):
        with ExitStack() as st2:
            e2 = st2.enter_context
            xt = [e2(nc.sbuf_tensor(f"{tag}lnB_x{k}", [128, D], F32)) for k in (0, 1)]
            xnb = [e2(nc.sbuf_tensor(f"{tag}lnB_n{k}", [128, D], BF16)) for k in (0, 1)]
            jk = e2(nc.sbuf_tensor(tag + "lnB_j", [128, D], F32))
            sc = e2(nc.sbuf_tensor(tag + "lnB_s", [128, 8], F32))
            ptr = e2(nc.psum_tensor(tag + "lnB_p", [128, 1024], F32))
            if gate:
                xtTf = e2(nc.sbuf_tensor(tag + "lnB_tf", [128, D], F32))
                g8 = e2(nc.sbuf_tensor(tag + "lnB_g8", [8, 128], F32))
                afA = e2(nc.sbuf_tensor(tag + "lnB_aA", [128, 8], F32))
                afB = e2(nc.sbuf_tensor(tag + "lnB_aB", [128, 8], F32))
                ptrF = e2(nc.psum_tensor(tag + "lnB_pF", [128, 1024], F32))
                pgF = e2(nc.psum_tensor(tag + "lnB_pg", [8, 128], F32))
                paF = e2(nc.psum_tensor(tag + "lnB_pa", [128, 8], F32))
            LNV = 9 if gate else 6
            with nc.Block() as blk:
                bl = [s["l0"].n, s["l1"].n]
                bw = [s["w0"].n, s["w1"].n]
                bac, bve1, bve2, bpe = s["ac1"].n, s["ve1"].n, s["ve2"].n, s["pe1"].n
                bpe2, bm1, bm2 = s["pe2"].n, s["m1"].n, s["m2"].n

                @blk.sync
                def _(sy):
                    for t in range(ntile):
                        if t >= 2:
                            sy.wait_ge(s["ve2"].h, bve2 + (1 if t - 2 == 0 else 3 * (t - 2)))
                            sy.wait_ge(s["ac1"].h, bac + 2 * (t - 2) + 1)
                            if gate:
                                sy.wait_ge(s["pe2"].h, bpe2 + 3 * (t - 2) + 1)
                        s["l%d" % (t % 2)].inc(sy.dma_start(
                            out=xt[t % 2][:], in_=xdram[128 * t:128 * (t + 1), :]), 16)
                        if out_tm is not None and t >= 1:
                            sy.wait_ge(s["ve2"].h, bve2 + (1 if t - 1 == 0 else 3 * (t - 1)))
                            s["w%d" % ((t - 1) % 2)].inc(sy.dma_start(
                                out=out_tm[128 * (t - 1):128 * t, :], in_=xnb[(t - 1) % 2][:]), 16)
                    if out_tm is not None:
                        t = ntile - 1
                        sy.wait_ge(s["ve2"].h, bve2 + 3 * t)
                        s["w%d" % (t % 2)].inc(sy.dma_start(
                            out=out_tm[128 * t:128 * (t + 1), :], in_=xnb[t % 2][:]), 16)

                @blk.scalar
                def _(a):
                    for t in range(ntile):
                        a.wait_ge(s["l%d" % (t % 2)].h, bl[t % 2] + 16 * (t // 2 + 1))
                        if t >= 1:
                            a.wait_ge(s["ve1"].h, bve1 + LNV * (t - 1) + 3)
                            a.wait_ge(s["ac1"].h, bac + 2 * t)
                        s["ac1"].inc(a.activation(jk[:], xt[t % 2][:], AF.Square,
                                                  accum_out=sc[:, 1:2]))
                        a.wait_ge(s["ve1"].h, bve1 + LNV * t + 5)
                        s["ac1"].inc(a.activation(sc[:, 4:5], sc[:, 3:4], AF.Sqrt,
                                                  bias=C.epscol[:]))

                @blk.vector
                def _(v):
                    for t in range(ntile):
                        v.wait_ge(s["l%d" % (t % 2)].h, bl[t % 2] + 16 * (t // 2 + 1))
                        if out_tm is not None and t >= 2:
                            v.wait_ge(s["w%d" % (t % 2)].h, bw[t % 2] + 16 * (t // 2))
                        s["ve1"].inc(v.reduce_sum(sc[:, 0:1], xt[t % 2][:], axis=AX))
                        v.wait_ge(s["ve1"].h, s["ve1"].n)
                        s["ve1"].inc(v.tensor_scalar(sc[:, 2:3], sc[:, 0:1], 1.0 / D, None, op0=OP.mult))
                        v.wait_ge(s["ve1"].h, s["ve1"].n)
                        v.wait_ge(s["ac1"].h, bac + 2 * t + 1)
                        s["ve1"].inc(v.tensor_scalar(sc[:, 3:4], sc[:, 1:2], 1.0 / D, None, op0=OP.mult))
                        v.wait_ge(s["ve1"].h, s["ve1"].n)
                        s["ve1"].inc(v.tensor_scalar(sc[:, 6:7], sc[:, 2:3], sc[:, 2:3], None, op0=OP.mult))
                        v.wait_ge(s["ve1"].h, s["ve1"].n)
                        s["ve1"].inc(v.tensor_tensor(sc[:, 3:4], sc[:, 3:4], sc[:, 6:7], op=OP.subtract))
                        v.wait_ge(s["ve1"].h, s["ve1"].n)
                        v.wait_ge(s["ac1"].h, bac + 2 * t + 2)
                        s["ve1"].inc(v.reciprocal(sc[:, 5:6], sc[:, 4:5]))
                        v.wait_ge(s["ve1"].h, s["ve1"].n)
                        if gate:
                            s["ve1"].inc(v.tensor_tensor(sc[:, 7:8], sc[:, 2:3], sc[:, 5:6],
                                                         op=OP.mult))
                            v.wait_ge(s["ve1"].h, s["ve1"].n)
                        s["ve2"].inc(v.tensor_scalar(xnb[t % 2][:], xt[t % 2][:], sc[:, 2:3],
                                                     sc[:, 5:6], op0=OP.subtract, op1=OP.mult))
                        v.wait_ge(s["ve2"].h, s["ve2"].n)
                        if t >= 1:
                            v.wait_ge(s["pe1"].h, bpe + t)
                            tc0 = (128 * (t - 1))
                            s["ve2"].inc(v.tensor_copy(dstT[:, 0:4, tc0:tc0 + 128], ptr[:, 0:512]))
                            s["ve2"].inc(v.tensor_copy(dstT[:, 4:8, tc0:tc0 + 128], ptr[:, 512:1024]))
                            v.wait_ge(s["ve2"].h, s["ve2"].n)
                        if gate:
                            v.wait_ge(s["pe2"].h, bpe2 + 3 * t + 1)
                            s["m1"].inc(v.tensor_copy(xtTf[:], ptrF[:]))
                            v.wait_ge(s["m1"].h, s["m1"].n)
                            v.wait_ge(s["pe2"].h, bpe2 + 3 * t + 2)
                            s["m1"].inc(v.tensor_copy(g8[0:8, :], pgF[:]))
                            v.wait_ge(s["m1"].h, s["m1"].n)
                            v.wait_ge(s["pe2"].h, bpe2 + 3 * t + 3)
                            s["ve1"].inc(v.tensor_scalar(afA[:], paF[:], sc[:, 5:6], None,
                                                         op0=OP.mult))
                            v.wait_ge(s["ve1"].h, s["ve1"].n)
                            s["ve1"].inc(v.tensor_scalar(afB[:], C.gwsB[:], sc[:, 7:8], None,
                                                         op0=OP.mult))
                            v.wait_ge(s["ve1"].h, s["ve1"].n)
                            s["m2"].inc(v.tensor_tensor(afl[:, t, :], afA[:], afB[:],
                                                        op=OP.subtract))
                            v.wait_ge(s["m2"].h, s["m2"].n)
                    v.wait_ge(s["pe1"].h, bpe + ntile)
                    tc0 = 128 * (ntile - 1)
                    s["ve2"].inc(v.tensor_copy(dstT[:, 0:4, tc0:tc0 + 128], ptr[:, 0:512]))
                    s["ve2"].inc(v.tensor_copy(dstT[:, 4:8, tc0:tc0 + 128], ptr[:, 512:1024]))
                    v.wait_ge(s["ve2"].h, s["ve2"].n)

                @blk.tensor
                def _(t_):
                    for t in range(ntile):
                        t_.wait_ge(s["ve2"].h, bve2 + (1 if t == 0 else 3 * t + 1))
                        for o in range(8):
                            ins = t_.matmul(ptr[:, 128 * o:128 * (o + 1)],
                                            xnb[t % 2][:, 128 * o:128 * (o + 1)],
                                            C.ident[:], start=True, stop=True)
                            if o == 7:
                                s["pe1"].inc(ins)
                        if gate:
                            if t >= 1:
                                t_.wait_ge(s["m1"].h, bm1 + 2 * (t - 1) + 1)
                            for o in range(8):
                                ins = t_.matmul(ptrF[:, 128 * o:128 * (o + 1)],
                                                xt[t % 2][:, 128 * o:128 * (o + 1)],
                                                C.identF[:], start=True, stop=True)
                                if o == 7:
                                    s["pe2"].inc(ins)
                            t_.wait_ge(s["m1"].h, bm1 + 2 * t + 1)
                            for k in range(8):
                                ins = t_.matmul(pgF[:], C.gwF[:, k, :],
                                                xtTf[:, 128 * k:128 * (k + 1)],
                                                start=(k == 0), stop=(k == 7))
                                if k == 7:
                                    s["pe2"].inc(ins)
                            t_.wait_ge(s["m1"].h, bm1 + 2 * t + 2)
                            if t >= 1:
                                t_.wait_ge(s["m2"].h, bm2 + t)
                            s["pe2"].inc(t_.matmul(paF[:], g8[0:8, :], C.identF[0:8, 0:8],
                                                   start=True, stop=True))

    emit_ln(C.x1f, 32, xn1T, C.xn1full, gate=True)
    with ExitStack() as st:
        e = st.enter_context
        asc = e(nc.sbuf_tensor("sfB_s", [128, 4], F32))
        aex = e(nc.sbuf_tensor("sfB_e", [128, 8], F32))
        afft = [e(nc.sbuf_tensor(f"sfB_f{k}", [128, 8], F32)) for k in (0, 1)]
        with nc.Block() as blk:
            bac, bve1, bve2 = s["ac1"].n, s["ve1"].n, s["ve2"].n
            bw = [s["w0"].n, s["w1"].n]

            @blk.vector
            def _(v):
                for t in range(32):
                    if t >= 1:
                        v.wait_ge(s["ac1"].h, bac + t)
                    s["ve1"].inc(v.reduce_max(asc[:, 0:1], afl[:, t, :], axis=AX))
                    v.wait_ge(s["ve1"].h, s["ve1"].n)
                    s["ve1"].inc(v.tensor_scalar(asc[:, 1:2], asc[:, 0:1], -1.0, None,
                                                 op0=OP.mult))
                    v.wait_ge(s["ve1"].h, s["ve1"].n)
                    v.wait_ge(s["ac1"].h, bac + t + 1)
                    s["ve1"].inc(v.reduce_sum(asc[:, 2:3], aex[:], axis=AX))
                    v.wait_ge(s["ve1"].h, s["ve1"].n)
                    s["ve1"].inc(v.reciprocal(asc[:, 3:4], asc[:, 2:3]))
                    v.wait_ge(s["ve1"].h, s["ve1"].n)
                    if t >= 2:
                        v.wait_ge(s["w%d" % (t % 2)].h, bw[t % 2] + 16 * (t // 2))
                    s["ve2"].inc(v.tensor_scalar(afft[t % 2][:], aex[:], asc[:, 3:4], None,
                                                 op0=OP.mult))
                    v.wait_ge(s["ve2"].h, s["ve2"].n)

            @blk.scalar
            def _(a):
                for t in range(32):
                    a.wait_ge(s["ve1"].h, bve1 + 4 * t + 2)
                    if t >= 1:
                        a.wait_ge(s["ve2"].h, bve2 + t)
                    s["ac1"].inc(a.activation(aex[:], afl[:, t, :], AF.Exp,
                                              bias=asc[:, 1:2]))

            @blk.sync
            def _(sy):
                for t in range(32):
                    sy.wait_ge(s["ve2"].h, bve2 + t + 1)
                    s["w%d" % (t % 2)].inc(sy.dma_start(
                        out=C.afful[128 * t:128 * (t + 1), :], in_=afft[t % 2][:]), 16)
                sy.wait_ge(s["w0"].h, s["w0"].n)
                sy.wait_ge(s["w1"].h, s["w1"].n)
    afls.close()

    emit_ln(C.x1q, 16, xnqT, None, tag="q")
    ph_l1_qkv(C, xn1T, xnqT, xnqs)
    ph_l1_attn(C)
    finish(C)
    return nc, C


def ph_l1_qkv(C, xn1T, xnqT, xnqs):
    nc, s = C.nc, C.s

    def proj_stage(orng, ntc, xsT, dst_of, mxdst, scale_ap=None, tag=""):
        with ExitStack() as st:
            e = st.enter_context
            wl = e(nc.sbuf_tensor(tag + "ps_wl", [128, 2, 8, 128], BF16))
            ev = [e(nc.sbuf_tensor(f"{tag}ps_e{k}", [128, 512], BF16)) for k in (0, 1)]
            sqc = e(nc.sbuf_tensor(tag + "ps_sq", [128, 512], BF16))
            pm = [e(nc.psum_tensor(f"{tag}ps_pm{k}", [128, 512], F32)) for k in (0, 1)]
            psq = e(nc.psum_tensor(tag + "ps_psq", [2, 512], F32))
            nch = len(orng) * ntc
            with nc.Block() as blk:
                bq = [s["q0"].n, s["q1"].n]
                bw = [s["w0"].n, s["w1"].n]
                bpe2, bpe3 = s["pe2"].n, s["pe3"].n
                bac2, bve3 = s["ac2"].n, s["ve3"].n
                chunks = [(i, o, tc) for i, o in enumerate(orng) for tc in range(ntc)]

                @blk.gpsimd
                def _(g):
                    for i, o in enumerate(orng):
                        if i >= 2:
                            g.wait_ge(s["pe2"].h, bpe2 + ntc * (i - 1))
                        s["q%d" % (i % 2)].inc(g.dma_start(out=wl[:, i % 2, :, :],
                                                           in_=C.wqkv1t[o]), 16)

                def sqmm(t_, cj):
                    t_.wait_ge(s["ac2"].h, bac2 + 2 * cj + 2)
                    if cj >= 1:
                        t_.wait_ge(s["ve3"].h, bve3 + cj)
                    s["pe3"].inc(t_.matmul(psq[:], C.bones[:], sqc[:, :],
                                           start=True, stop=True))

                @blk.tensor
                def _(t_):
                    for ci, (i, o, tc) in enumerate(chunks):
                        t_.wait_ge(s["q%d" % (i % 2)].h, bq[i % 2] + 16 * (i // 2 + 1))
                        if ci >= 2:
                            t_.wait_ge(s["ac2"].h, bac2 + 2 * ci - 3)
                        for k in range(8):
                            ins = t_.matmul(pm[ci % 2][:], wl[:, i % 2, k, :],
                                            xsT[:, k, tc * 512:tc * 512 + 512],
                                            start=(k == 0), stop=(k == 7))
                            if k == 7:
                                s["pe2"].inc(ins)
                        if ci >= 1:
                            sqmm(t_, ci - 1)
                    sqmm(t_, nch - 1)

                @blk.scalar
                def _(a):
                    for ci in range(nch):
                        a.wait_ge(s["pe2"].h, bpe2 + ci + 1)
                        if ci >= 2:
                            a.wait_ge(s["w%d" % (ci % 2)].h, bw[ci % 2] + 16 * (ci // 2))
                        if scale_ap is not None:
                            s["ac2"].inc(a.activation(ev[ci % 2][:], pm[ci % 2][:],
                                                      AF.Identity, bias=C.zeroc[:],
                                                      scale=scale_ap[:]))
                        else:
                            s["ac2"].inc(a.activation(ev[ci % 2][:], pm[ci % 2][:],
                                                      AF.Identity, bias=C.zeroc[:]))
                        if ci >= 1:
                            a.wait_ge(s["pe3"].h, bpe3 + ci)
                        s["ac2"].inc(a.activation(sqc[:, :], ev[ci % 2][:], AF.Square))

                @blk.vector
                def _(v):
                    for cj in range(nch):
                        v.wait_ge(s["pe3"].h, bpe3 + cj + 1)
                        s["ve3"].inc(v.reduce_max(mxdst[:, cj:cj + 1], psq[:], axis=AX))
                        v.wait_ge(s["ve3"].h, s["ve3"].n)

                @blk.sync
                def _(sy):
                    for ci, (i, o, tc) in enumerate(chunks):
                        sy.wait_ge(s["ac2"].h, bac2 + 2 * ci + 1)
                        s["w%d" % (ci % 2)].inc(sy.dma_start(
                            out=dst_of(i, tc), in_=ev[ci % 2][:]), 16)
                    sy.wait_ge(s["w0"].h, s["w0"].n)
                    sy.wait_ge(s["w1"].h, s["w1"].n)

    proj_stage(range(8), 4, xnqT,
               lambda i, tc: C.qstg[:, i, 512 * tc:512 * (tc + 1)],
               C.mxq, scale_ap=C.qscl, tag="q")
    xnqs.close()

    with ExitStack() as st:
        e = st.enter_context
        vscr = e(nc.sbuf_tensor("vB_s", [128, S], BF16))
        wl = e(nc.sbuf_tensor("vB_wl", [128, 2, 8, 128], BF16))
        pm = [e(nc.psum_tensor(f"vB_pm{k}", [128, 512], F32)) for k in (0, 1)]
        pv = [e(nc.psum_tensor(f"vB_pv{k}", [128, 512], F32)) for k in (0, 1)]
        with nc.Block() as blk:
            bq = [s["q0"].n, s["q1"].n]
            bpe2, bpe3 = s["pe2"].n, s["pe3"].n
            bac2, bve3, bgp1 = s["ac2"].n, s["ve3"].n, s["gp1"].n

            @blk.gpsimd
            def _(g):
                vv = C.vtok1.rearrange("p a (h c) -> p a h c", c=65)
                s["gp1"].inc(g.memset(vv[:, :, :, 64:65], 1.0))
                for i in range(8):
                    if i >= 2:
                        g.wait_ge(s["pe2"].h, bpe2 + 8 * (i - 1))
                    s["q%d" % (i % 2)].inc(g.dma_start(out=wl[:, i % 2, :, :],
                                                       in_=C.wqkv1t[16 + i]), 16)

            def transp(t_, cj):
                pvi, ptc = divmod(cj, 8)
                t_.wait_ge(s["ac2"].h, bac2 + cj + 1)
                if cj >= 2:
                    t_.wait_ge(s["ve3"].h, bve3 + cj - 1)
                for jj in range(4):
                    ins = t_.matmul(pv[cj % 2][:, 128 * jj:128 * (jj + 1)],
                                    vscr[:, 512 * ptc + 128 * jj:512 * ptc + 128 * (jj + 1)],
                                    C.ident[:], start=True, stop=True)
                    if jj == 3:
                        s["pe3"].inc(ins)

            @blk.tensor
            def _(t_):
                for ci in range(64):
                    vi, tc = divmod(ci, 8)
                    t_.wait_ge(s["q%d" % (vi % 2)].h, bq[vi % 2] + 16 * (vi // 2 + 1))
                    if ci >= 2:
                        t_.wait_ge(s["ac2"].h, bac2 + ci - 1)
                    for k in range(8):
                        ins = t_.matmul(pm[ci % 2][:], wl[:, vi % 2, k, :],
                                        xn1T[:, k, tc * 512:tc * 512 + 512],
                                        start=(k == 0), stop=(k == 7))
                        if k == 7:
                            s["pe2"].inc(ins)
                    if ci >= 1:
                        transp(t_, ci - 1)
                transp(t_, 63)

            @blk.scalar
            def _(a):
                for ci in range(64):
                    vi, tc = divmod(ci, 8)
                    a.wait_ge(s["pe2"].h, bpe2 + ci + 1)
                    if ci >= 8:
                        a.wait_ge(s["pe3"].h, bpe3 + ci - 7)
                    s["ac2"].inc(a.activation(vscr[:, 512 * tc:512 * (tc + 1)],
                                              pm[ci % 2][:], AF.Identity, bias=C.zeroc[:]))

            @blk.vector
            def _(v):
                dst = C.vtok1.rearrange("p a (h c) -> p a h c", c=65)
                v.wait_ge(s["gp1"].h, bgp1 + 1)
                for cj in range(64):
                    pvi, ptc = divmod(cj, 8)
                    v.wait_ge(s["pe3"].h, bpe3 + cj + 1)
                    s["ve3"].inc(v.tensor_copy(
                        dst[:, ptc * 4:(ptc + 1) * 4, 2 * pvi:2 * pvi + 2, 0:64],
                        pv[cj % 2][:].rearrange("p (a h c) -> p a h c", a=4, h=2)))
                    v.wait_ge(s["ve3"].h, s["ve3"].n)

    proj_stage(range(8, 16), 8, xn1T,
               lambda i, tc: C.kst[:, i, 512 * tc:512 * (tc + 1)],
               C.mxk, tag="k")


def ph_l1_attn(C):
    nc, s = C.nc, C.s
    C.lnstack.close()
    qasm = C.ent(nc.sbuf_tensor("qasm", [128, 8, OWN], BF16))
    kslc = C.ent(nc.sbuf_tensor("kslc", [128, 8, S], BF16))
    mask8 = C.ent(nc.sbuf_tensor("mask8", [128, 8, 512], BF16))
    posq = C.ent(nc.sbuf_tensor("posq", [1, 4, 512], F32))
    SLB = [L1_NH * sum(EXT1[:i]) for i in range(4)]

    with ExitStack() as st:
        e = st.enter_context
        mx2 = e(nc.sbuf_tensor("prB_m2", [2, 2], F32))
        mx2b = e(nc.sbuf_tensor("prB_mb", [2, 2], BF16))
        mcol = e(nc.sbuf_tensor("prB_mc", [2, 1], F32))
        mrb = e(nc.sbuf_tensor("prB_mrb", [2, 1], BF16))
        mrow = e(nc.sbuf_tensor("prB_mr", [1, 6], F32))
        ptt = e(nc.psum_tensor("prB_pt", [2, 2], F32))
        pbc = e(nc.psum_tensor("prB_pb", [128, 1], F32))
        with nc.Block() as blk:
            bm1, bm2, bm3 = s["m1"].n, s["m2"].n, s["m3"].n

            @blk.sync
            def _(sy):
                s["dB"].inc(sy.dma_start(out=qasm[:, :, :], in_=C.qstg), 16)
                s["dB"].inc(sy.dma_start(out=posq[0:1, :, :], in_=C.posr), 16)

            @blk.vector
            def _(v):
                s["m3"].inc(v.reduce_max(mx2[:, 0:1], C.mxk[:, :], axis=AX))
                v.wait_ge(s["m3"].h, s["m3"].n)
                s["m3"].inc(v.reduce_max(mx2[:, 1:2], C.mxq[:, :], axis=AX))
                v.wait_ge(s["m3"].h, s["m3"].n)
                s["m3"].inc(v.tensor_copy(mx2b[:, :], mx2[:, :]))
                v.wait_ge(s["m3"].h, s["m3"].n)
                v.wait_ge(s["m1"].h, bm1 + 1)
                s["m3"].inc(v.reduce_max(mcol[:, 0:1], ptt[:, 0:2], axis=AX))
                v.wait_ge(s["m3"].h, s["m3"].n)
                s["m3"].inc(v.tensor_copy(mrb[:, 0:1], mcol[:, 0:1]))
                v.wait_ge(s["m3"].h, s["m3"].n)
                v.wait_ge(s["m1"].h, bm1 + 2)
                s["m3"].inc(v.tensor_copy(mrow[0:1, 0:2], ptt[0:1, 0:2]))
                v.wait_ge(s["m3"].h, s["m3"].n)
                s["m3"].inc(v.tensor_tensor(mrow[0:1, 2:3], mrow[0:1, 0:1],
                                            mrow[0:1, 1:2], op=OP.mult))
                v.wait_ge(s["m3"].h, s["m3"].n)
                v.wait_ge(s["m2"].h, bm2 + 1)
                s["m3"].inc(v.tensor_scalar(mrow[0:1, 4:5], mrow[0:1, 3:4], -1.0,
                                            None, op0=OP.mult))
                v.wait_ge(s["m3"].h, s["m3"].n)
                v.wait_ge(s["m1"].h, bm1 + 3)
                s["m3"].inc(v.tensor_copy(C.mnegc[:], pbc[:]))
                v.wait_ge(s["m3"].h, s["m3"].n)

            @blk.tensor
            def _(t_):
                t_.wait_ge(s["m3"].h, bm3 + 3)
                s["m1"].inc(t_.matmul(ptt[:], mx2b[:], C.ident[0:2, 0:2],
                                      start=True, stop=True))
                t_.wait_ge(s["m3"].h, bm3 + 5)
                s["m1"].inc(t_.matmul(ptt[0:1, 0:2], mrb[:, 0:1], C.ident[0:2, 0:2],
                                      start=True, stop=True))
                t_.wait_ge(s["m3"].h, bm3 + 8)
                s["m1"].inc(t_.matmul(pbc[:], C.onesFr[:], mrow[0:1, 4:5],
                                      start=True, stop=True))

            @blk.scalar
            def _(a):
                a.wait_ge(s["m3"].h, bm3 + 7)
                s["m2"].inc(a.activation(mrow[0:1, 3:4], mrow[0:1, 2:3], AF.Sqrt,
                                         bias=C.zeroc[0:1]))

    with ExitStack() as st:
        e = st.enter_context
        psb = [e(nc.sbuf_tensor(f"atB_p{k}", [128, 512], BF16)) for k in (0, 1)]
        sAV = [e(nc.sbuf_tensor(f"atB_s{k}", [64, 512], BF16)) for k in (0, 1)]
        zbuf = [e(nc.sbuf_tensor(f"atB_z{k}", [1, 512], F32)) for k in (0, 1)]
        pl = [e(nc.psum_tensor(f"atB_l{k}", [128, 512], F32)) for k in (0, 1)]
        pav = [e(nc.psum_tensor(f"atB_a{k}", [65, 512], F32)) for k in (0, 1)]
        pmb = e(nc.psum_tensor("atB_pm", [128, 512], F32))
        with nc.Block() as blk:
            bdB = s["dB"].n
            bpe1, bpe2, bpe3 = s["pe1"].n, s["pe2"].n, s["pe3"].n
            bac1, bac2 = s["ac1"].n, s["ac2"].n
            bve1, bve2, bve3 = s["ve1"].n, s["ve2"].n, s["ve3"].n
            bw = [s["w0"].n, s["w1"].n]
            bl = [s["l0"].n, s["l1"].n]

            @blk.sync
            def _(sy):
                sy.wait_ge(s["dB"].h, bdB)
                for sl in range(4):
                    if sl >= 1:
                        sy.wait_ge(s["pe1"].h, bpe1 + SLB[sl])
                    s["dB"].inc(sy.dma_start(
                        out=kslc[:, :, 0:128 * EXT1[sl]],
                        in_=C.kst[:, :, 0:128 * EXT1[sl]]), 16)
                    for h in range(L1_NH):
                        t, u = h // 2, h % 2
                        idx = L1_NH * sl + h
                        sy.wait_ge(s["ac2"].h, bac2 + idx + 1)
                        s["w%d" % (idx % 2)].inc(sy.dma_start(
                            out=C.s1ch[64 * u:64 * u + 64, t, 512 * sl:512 * (sl + 1)],
                            in_=sAV[idx % 2][:]), 16)
                        sy.wait_ge(s["ve2"].h, bve2 + idx + 1)
                        s["l%d" % (idx % 2)].inc(sy.dma_start(
                            out=C.zst[h, sl, :], in_=zbuf[idx % 2][0:1, :]), 16)
                sy.wait_ge(s["w0"].h, s["w0"].n)
                sy.wait_ge(s["w1"].h, s["w1"].n)
                sy.wait_ge(s["l0"].h, s["l0"].n)
                sy.wait_ge(s["l1"].h, s["l1"].n)

            @blk.tensor
            def _(t_):
                for sl in range(4):
                    E = EXT1[sl]
                    if sl >= 1:
                        t_.wait_ge(s["ve3"].h, bve3 + 8 * sl)
                    else:
                        t_.wait_ge(s["dB"].h, bdB)
                    s["pe3"].inc(t_.matmul(pmb[:], C.onesFr[:], posq[0:1, sl, :],
                                           start=True, stop=True))
                    for h in range(L1_NH):
                        t, u = h // 2, h % 2
                        idx = L1_NH * sl + h
                        cE = SLB[sl] + h * E

                        def qk(jt):
                            ee = cE + jt
                            if ee >= 2:
                                t_.wait_ge(s["ac1"].h, bac1 + ee - 1)
                            if h == 0 and jt == 0:
                                t_.wait_ge(s["dB"].h, bdB + 16 * (sl + 1))
                            ins = t_.matmul(pl[ee % 2][:],
                                            kslc[64 * u:64 * u + 64, t,
                                                 128 * jt:128 * (jt + 1)],
                                            qasm[64 * u:64 * u + 64, t,
                                                 512 * sl:512 * (sl + 1)],
                                            start=True, stop=True)
                            s["pe1"].inc(ins)

                        def av(jt):
                            ee = cE + jt
                            if jt >= FULL1[sl]:
                                t_.wait_ge(s["ve1"].h,
                                           bve1 + 8 * idx + (jt - FULL1[sl]) + 1)
                            else:
                                t_.wait_ge(s["ac1"].h, bac1 + ee + 1)
                            if jt == 0 and idx >= 2:
                                t_.wait_ge(s["ac2"].h, bac2 + idx - 1)
                                t_.wait_ge(s["ve2"].h, bve2 + idx - 1)
                            ins = t_.matmul(pav[idx % 2][:],
                                            C.vtok1[:, jt, 65 * h:65 * h + 65],
                                            psb[ee % 2][:],
                                            start=(jt == 0), stop=(jt == E - 1))
                            s["pe2"].inc(ins)

                        qk(0)
                        for jt in range(1, E):
                            qk(jt)
                            av(jt - 1)
                        av(E - 1)

            @blk.scalar
            def _(a):
                for sl in range(4):
                    E = EXT1[sl]
                    for h in range(L1_NH):
                        idx = L1_NH * sl + h
                        cE = SLB[sl] + h * E
                        for jt in range(E):
                            ee = cE + jt
                            a.wait_ge(s["pe1"].h, bpe1 + ee + 1)
                            if ee >= 2:
                                a.wait_ge(s["pe2"].h, bpe2 + ee - 1)
                            s["ac1"].inc(a.activation(psb[ee % 2][:], pl[ee % 2][:],
                                                      AF.Exp, bias=C.mnegc[:]))
                        a.wait_ge(s["pe2"].h, bpe2 + cE + E)
                        if idx >= 2:
                            a.wait_ge(s["w%d" % (idx % 2)].h, bw[idx % 2] + 16 * (idx // 2))
                        s["ac2"].inc(a.activation(sAV[idx % 2][:], pav[idx % 2][0:64, :],
                                                  AF.Identity, bias=C.zeroc[0:64]))

            @blk.vector
            def _(v):
                for sl in range(4):
                    E = EXT1[sl]
                    v.wait_ge(s["pe3"].h, bpe3 + sl + 1)
                    for k2 in range(8):
                        jt = FULL1[sl] + k2
                        s["ve3"].inc(v.tensor_scalar(mask8[:, k2, :], pmb[:],
                                                     C.jcolv[:, jt:jt + 1], None,
                                                     op0=OP.is_ge))
                        v.wait_ge(s["ve3"].h, s["ve3"].n)
                    for h in range(L1_NH):
                        idx = L1_NH * sl + h
                        cE = SLB[sl] + h * E
                        for k2 in range(8):
                            jt = FULL1[sl] + k2
                            ee = cE + jt
                            v.wait_ge(s["ac1"].h, bac1 + ee + 1)
                            s["ve1"].inc(v.tensor_tensor(psb[ee % 2][:], psb[ee % 2][:],
                                                         mask8[:, k2, :], op=OP.mult))
                            v.wait_ge(s["ve1"].h, s["ve1"].n)
                        v.wait_ge(s["pe2"].h, bpe2 + cE + E)
                        if idx >= 2:
                            v.wait_ge(s["l%d" % (idx % 2)].h, bl[idx % 2] + 16 * (idx // 2))
                        s["ve2"].inc(v.tensor_copy(zbuf[idx % 2][0:1, :],
                                                   pav[idx % 2][64:65, :]))
                        v.wait_ge(s["ve2"].h, s["ve2"].n)


L1B_NB = 4


def build_l1b():
    nc = bass.Bass("TRN2", target_bir_lowering=False,
                   dynamic_dma_scratch_size=28672)
    C = Ctx()
    C.nc, C.stack = nc, []

    def ent(cm):
        C.stack.append(cm)
        return cm.__enter__()

    C.ent = ent
    C.affs = nc.dram_tensor("affs", [4, S], F32, kind="ExternalInput").ap()
    C.xnall = nc.dram_tensor("xnall", [4, S, D], BF16, kind="ExternalInput").ap()
    C.wgt = nc.dram_tensor("wgt", [32, 128, 8, 128], F32, kind="ExternalInput").ap()
    C.wut = nc.dram_tensor("wut", [32, 128, 8, 128], F32, kind="ExternalInput").ap()
    C.wdt = nc.dram_tensor("wdt", [8, 128, 32, 128], F32, kind="ExternalInput").ap()
    C.identD = nc.dram_tensor("identD", [128, 128], BF16, kind="ExternalInput").ap()
    C.trilD = nc.dram_tensor("trilD", [128, 128], BF16, kind="ExternalInput").ap()

    C.partial = nc.dram_tensor("partial", [4, S, D], BF16, kind="ExternalOutput").ap()

    C.ident = ent(nc.sbuf_tensor("ident", [128, 128], BF16))
    C.tril = ent(nc.sbuf_tensor("tril", [128, 128], BF16))
    C.onesF128 = ent(nc.sbuf_tensor("onesF128", [128, 1], F32))
    C.onesB128 = ent(nc.sbuf_tensor("onesB128", [128, 1], BF16))
    C.onesFr = ent(nc.sbuf_tensor("onesFr", [1, 128], F32))
    C.zrow = ent(nc.sbuf_tensor("zrow", [128, 4096], BF16))
    C.zeroF = ent(nc.sbuf_tensor("zeroF", [128, 1], F32))
    C.identF = ent(nc.sbuf_tensor("identF", [128, 128], F32))
    C.jrowD = ent(nc.sbuf_tensor("jrowD", [128, 8, 128], F32))
    C.cvals = ent(nc.sbuf_tensor("cvals", [128, 32], F32))
    C.pm32 = ent(nc.sbuf_tensor("pm32", [128, 32], F32))
    C.pcol0 = ent(nc.sbuf_tensor("pcol0", [128, 1], F32))
    C.gall = ent(nc.sbuf_tensor("gall", [128, 4, 8], F32))
    C.tokuall = ent(nc.sbuf_tensor("tokuall", [128, 4, 8], U32))

    names = ["dA", "pe1", "pe2", "ac1", "ve1", "ve2", "gp1", "q0", "q1",
             "w0", "w1", "m1", "m3"]
    C.s = {k: Cnt(ent(nc.semaphore("s_" + k))) for k in names}
    s = C.s

    with nc.Block() as blk:
        @blk.sync
        def _(sy):
            s["dA"].inc(sy.dma_start(out=C.ident[:], in_=C.identD), 16)
            s["dA"].inc(sy.dma_start(out=C.tril[:], in_=C.trilD), 16)

        @blk.gpsimd
        def _(g):
            s["gp1"].inc(g.memset(C.onesF128[:], 1.0))
            s["gp1"].inc(g.memset(C.onesB128[:], 1.0))
            s["gp1"].inc(g.memset(C.onesFr[:], 1.0))
            s["gp1"].inc(g.memset(C.zrow[:], 0.0))
            s["gp1"].inc(g.memset(C.zeroF[:], 0.0))
            s["gp1"].inc(g.memset(C.pm32[:], 0.0))
            s["gp1"].inc(g.iota(C.pcol0[:], [[0, 1]], base=0, channel_multiplier=1,
                                allow_small_or_imprecise_dtypes=True))
            s["gp1"].inc(g.iota(C.cvals[:], [[1, 32]], base=0, channel_multiplier=0,
                                allow_small_or_imprecise_dtypes=True))
            for st2 in range(8):
                s["gp1"].inc(g.iota(C.jrowD[:, st2, :], [[1, 128]], base=128 * st2,
                                    channel_multiplier=0,
                                    allow_small_or_imprecise_dtypes=True))

        @blk.vector
        def _(v):
            v.wait_ge(s["dA"].h, s["dA"].n)
            s["ve1"].inc(v.tensor_copy(C.identF[:], C.ident[:]))
            v.wait_ge(s["ve1"].h, s["ve1"].n)
            v.wait_ge(s["gp1"].h, s["gp1"].n)
            s["ve1"].inc(v.tensor_scalar(C.pm32[:, :], C.pm32[:, :], C.pcol0[:],
                                         None, op0=OP.add))
            v.wait_ge(s["ve1"].h, s["ve1"].n)

    with nc.Block() as blk:
        bgp = s["gp1"].n

        @blk.sync
        def _(sy):
            sy.wait_ge(s["gp1"].h, bgp)
            pv = C.partial.rearrange("b (a p) f -> b p a f", p=128)
            for b in range(4):
                for a in range(8):
                    s["w0"].inc(sy.dma_start(out=pv[b, :, 4 * a:4 * a + 4, :],
                                             in_=C.zrow[:].rearrange(
                                                 "p (a f) -> p a f", a=4)), 16)
    C.zdone = s["w0"].n

    ph_moe_select(C)
    for b in range(L1B_NB):
        ph_moe_ffn(C, b)
    finish(C)
    return nc, C


def ph_moe_select(C):
    nc, s = C.nc, C.s
    NIT = 64
    with ExitStack() as st:
        e = st.enter_context
        avt1 = e(nc.sbuf_tensor("slB_a1", [32, 4, 128], F32))
        av4 = e(nc.sbuf_tensor("slB_a4", [128, 4, 32], F32))
        mskb = e(nc.sbuf_tensor("slB_mk", [128, 4, 32], BF16))
        stt = e(nc.sbuf_tensor("slB_st", [1, 32], F32))
        midc = e(nc.sbuf_tensor("slB_mc", [128, 4], F32))
        hic = e(nc.sbuf_tensor("slB_hc", [128, 4], F32))
        loc = e(nc.sbuf_tensor("slB_lc", [128, 4], F32))
        ndc = e(nc.sbuf_tensor("slB_nd", [128, 4], F32))
        selD = e(nc.sbuf_tensor("slB_sD", [128, 4, 32], F32))
        eqm = e(nc.sbuf_tensor("slB_eq", [128, 4, 32], F32))
        sel = e(nc.sbuf_tensor("slB_se", [128, 4, 32], F32))
        pfa1 = e(nc.sbuf_tensor("slB_p1", [1, 4, 32], F32))
        pfb1 = e(nc.sbuf_tensor("slB_p2", [1, 4, 32], F32))
        rkB = e(nc.sbuf_tensor("slB_rB", [128, 4, 32], F32))
        rank = e(nc.sbuf_tensor("slB_rk", [128, 4, 32], F32))
        slotv = e(nc.sbuf_tensor("slB_sv", [128, 4, 32], F32))
        soff = e(nc.sbuf_tensor("slB_so", [128, 4, 32], F32))
        ghiF = e(nc.sbuf_tensor("slB_gh", [128, 4, 32], F32))
        metac = e(nc.sbuf_tensor("slB_mt", [128, 4, 32, 4], BF16))
        Mt = e(nc.sbuf_tensor("slB_Mt", [128, 2, 8, 128], BF16))
        msb = e(nc.sbuf_tensor("slB_mb", [128, 8, 4], F32))
        tmp8 = e(nc.sbuf_tensor("slB_t8", [128, 8], F32))
        pav = e(nc.psum_tensor("slB_pv", [128, 32], F32))
        pmid = e(nc.psum_tensor("slB_pm", [128, 4], F32))
        pcnt = e(nc.psum_tensor("slB_pc", [1, 128], F32))
        ptrl = e(nc.psum_tensor("slB_pt", [128, 128], F32))
        pbcA = e(nc.psum_tensor("slB_pb", [128, 128], F32))
        pmeta = e(nc.psum_tensor("slB_pe", [128, 8, 4], F32))
        M = {}
        with nc.Block() as blk:
            bdA, bm1, bgp = s["dA"].n, s["m1"].n, s["gp1"].n

            @blk.sync
            def _(sy):
                s["dA"].inc(sy.dma_start(
                    out=avt1[:, :, :],
                    in_=C.affs.rearrange("b (a t) -> a b t", a=32)), 16)

            def V(v, ins, mark=None):
                s["m3"].inc(ins)
                v.wait_ge(s["m3"].h, s["m3"].n)
                if mark is not None:
                    M[mark] = s["m3"].n

            @blk.vector
            def _(v):
                v.wait_ge(s["gp1"].h, bgp + 2)
                for b in range(4):
                    v.wait_ge(s["m1"].h, bm1 + b + 1)
                    V(v, v.tensor_copy(av4[:, b, :], pav[:, :]), ("av", b))
                for it in range(NIT):
                    V(v, v.tensor_tensor(stt[0:1, 8:12], stt[0:1, 0:4],
                                         stt[0:1, 4:8], op=OP.add))
                    V(v, v.tensor_scalar(stt[0:1, 8:12], stt[0:1, 8:12], 0.5,
                                         None, op0=OP.mult), ("mid", it))
                    v.wait_ge(s["m1"].h, bm1 + 4 + 2 * it + 1)
                    V(v, v.tensor_copy(midc[:, :], pmid[:, :]))
                    for b in range(4):
                        V(v, v.tensor_scalar(mskb[:, b, :], av4[:, b, :],
                                             midc[:, b:b + 1], None, op0=OP.is_ge),
                          ("msk", it) if b == 3 else None)
                    v.wait_ge(s["m1"].h, bm1 + 4 + 2 * it + 2)
                    V(v, v.reduce_sum(stt[0:1, 12:16], pcnt[0:1, :].rearrange(
                        "one (b c) -> one b c", b=4), axis=AX))
                    V(v, v.tensor_scalar(stt[0:1, 16:20], stt[0:1, 12:16], 1024.0,
                                         None, op0=OP.is_ge))
                    V(v, v.tensor_tensor(stt[0:1, 20:24], stt[0:1, 8:12],
                                         stt[0:1, 0:4], op=OP.subtract))
                    V(v, v.tensor_tensor(stt[0:1, 20:24], stt[0:1, 20:24],
                                         stt[0:1, 16:20], op=OP.mult))
                    V(v, v.tensor_tensor(stt[0:1, 0:4], stt[0:1, 0:4],
                                         stt[0:1, 20:24], op=OP.add))
                    V(v, v.tensor_tensor(stt[0:1, 24:28], stt[0:1, 4:8],
                                         stt[0:1, 8:12], op=OP.subtract))
                    V(v, v.tensor_tensor(stt[0:1, 24:28], stt[0:1, 24:28],
                                         stt[0:1, 16:20], op=OP.mult))
                    V(v, v.tensor_tensor(stt[0:1, 4:8], stt[0:1, 8:12],
                                         stt[0:1, 24:28], op=OP.add), ("fin", it))
                NB = bm1 + 4 + 2 * NIT
                v.wait_ge(s["m1"].h, NB + 1)
                V(v, v.tensor_copy(hic[:, :], pmid[:, :]), "hic")
                v.wait_ge(s["m1"].h, NB + 2)
                V(v, v.tensor_copy(loc[:, :], pmid[:, :]))
                for b in range(4):
                    V(v, v.tensor_scalar(selD[:, b, :], av4[:, b, :],
                                         hic[:, b:b + 1], None, op0=OP.is_ge))
                    V(v, v.tensor_scalar(eqm[:, b, :], av4[:, b, :],
                                         loc[:, b:b + 1], None, op0=OP.is_ge))
                V(v, v.tensor_tensor(eqm[:, :, :], eqm[:, :, :], selD[:, :, :],
                                     op=OP.subtract))
                V(v, v.tensor_copy(mskb[:, :, :], selD[:, :, :]), "selDb")
                v.wait_ge(s["m1"].h, NB + 3)
                V(v, v.reduce_sum(stt[0:1, 12:16], pcnt[0:1, :].rearrange(
                    "one (b c) -> one b c", b=4), axis=AX))
                V(v, v.tensor_scalar(stt[0:1, 20:24], stt[0:1, 12:16], -1.0,
                                     1024.0, op0=OP.mult, op1=OP.add), "need")
                v.wait_ge(s["m1"].h, NB + 4)
                V(v, v.tensor_copy(ndc[:, :], pmid[:, :]))

                def rank_of(v, nb, dst, mk):
                    v.wait_ge(s["m1"].h, nb + 1)
                    V(v, v.tensor_copy(pfa1[0:1, :, :], pcnt[0:1, :].rearrange(
                        "one (b c) -> one b c", b=4)))
                    V(v, v.tensor_scalar(pfb1[0:1, :, 0:1], pfa1[0:1, :, 0:1],
                                         0.0, None, op0=OP.mult))
                    V(v, v.tensor_copy(pfb1[0:1, :, 1:32], pfa1[0:1, :, 0:31]))
                    cur, nxt = pfb1, pfa1
                    for k in (1, 2, 4, 8, 16):
                        V(v, v.tensor_copy(nxt[0:1, :, 0:k], cur[0:1, :, 0:k]))
                        V(v, v.tensor_tensor(nxt[0:1, :, k:32], cur[0:1, :, k:32],
                                             cur[0:1, :, 0:32 - k], op=OP.add))
                        cur, nxt = nxt, cur
                    V(v, v.tensor_copy(pfb1[0:1, :, :], cur[0:1, :, :]),
                      "pref_" + mk)
                    v.wait_ge(s["m1"].h, nb + 3)
                    V(v, v.tensor_copy(rkB[:, :, :], ptrl[:, :].rearrange(
                        "p (b c) -> p b c", b=4)))
                    V(v, v.tensor_tensor(dst[:, :, :], rkB[:, :, :],
                                         pbcA[:, :].rearrange(
                                             "p (b c) -> p b c", b=4), op=OP.add))

                V(v, v.tensor_copy(mskb[:, :, :], eqm[:, :, :]), "eqb")
                rank_of(v, NB + 4, rank, "eqb")
                for b in range(4):
                    V(v, v.tensor_scalar(rank[:, b, :], rank[:, b, :],
                                         ndc[:, b:b + 1], None, op0=OP.is_lt))
                V(v, v.tensor_tensor(rank[:, :, :], rank[:, :, :], eqm[:, :, :],
                                     op=OP.mult))
                V(v, v.tensor_tensor(sel[:, :, :], selD[:, :, :], rank[:, :, :],
                                     op=OP.add))
                V(v, v.tensor_copy(mskb[:, :, :], sel[:, :, :]), "selb")
                rank_of(v, NB + 7, slotv, "selb")
                V(v, v.tensor_scalar(soff[:, :, :], sel[:, :, :], -40000.0,
                                     40000.0, op0=OP.mult, op1=OP.add))
                V(v, v.tensor_tensor(slotv[:, :, :], slotv[:, :, :], sel[:, :, :],
                                     op=OP.mult))
                V(v, v.tensor_tensor(soff[:, :, :], soff[:, :, :], slotv[:, :, :],
                                     op=OP.add), "soff")
                V(v, v.tensor_copy(metac[:, :, :, 0:1].rearrange(
                    "p b c one -> p b (c one)"), av4[:, :, :]))
                V(v, v.tensor_copy(ghiF[:, :, :], metac[:, :, :, 0:1].rearrange(
                    "p b c one -> p b (c one)")))
                V(v, v.tensor_tensor(ghiF[:, :, :], av4[:, :, :], ghiF[:, :, :],
                                     op=OP.subtract))
                V(v, v.tensor_copy(metac[:, :, :, 1:2].rearrange(
                    "p b c one -> p b (c one)"), ghiF[:, :, :]))
                for b in range(4):
                    V(v, v.tensor_copy(metac[:, b, :, 2:3].rearrange(
                        "p c one -> p (c one)"), C.pm32[:, :]))
                    V(v, v.tensor_copy(metac[:, b, :, 3:4].rearrange(
                        "p c one -> p (c one)"), C.cvals[:, :]), ("metac", b))
                MMB = bm1 + 4 + 2 * NIT + 10
                for b in range(4):
                    for st2 in range(8):
                        for c in range(32):
                            gidx = 256 * b + 32 * st2 + c
                            if gidx >= 2:
                                v.wait_ge(s["m1"].h, MMB + gidx - 1)
                            V(v, v.tensor_scalar(Mt[:, c % 2, st2, :],
                                                 C.jrowD[:, st2, :],
                                                 soff[:, b, c:c + 1], None,
                                                 op0=OP.is_equal),
                              ("mt", b, st2, c))
                    v.wait_ge(s["m1"].h, MMB + 256 * (b + 1))
                    V(v, v.tensor_copy(msb[:, :, :], pmeta[:, :, :]), ("ev", b))
                    V(v, v.tensor_tensor(C.gall[:, b, :], msb[:, :, 0:1].rearrange(
                        "p t one -> p (t one)"), msb[:, :, 1:2].rearrange(
                        "p t one -> p (t one)"), op=OP.add))
                    V(v, v.tensor_scalar(tmp8[:, :], msb[:, :, 3:4].rearrange(
                        "p t one -> p (t one)"), 128.0, 4096.0 * b,
                        op0=OP.mult, op1=OP.add))
                    V(v, v.tensor_tensor(tmp8[:, :], tmp8[:, :],
                                         msb[:, :, 2:3].rearrange(
                                             "p t one -> p (t one)"), op=OP.add))
                    V(v, v.tensor_copy(C.tokuall[:, b, :], tmp8[:, :]))
                M["done"] = s["m3"].n

            @blk.tensor
            def _(t_):
                t_.wait_ge(s["dA"].h, bdA + 16)
                for b in range(4):
                    if b >= 1:
                        t_.wait_ge(s["m3"].h, M[("av", b - 1)])
                    s["m1"].inc(t_.matmul(pav[:, :], avt1[:, b, :],
                                          C.identF[0:32, 0:32],
                                          start=True, stop=True))
                for it in range(NIT):
                    t_.wait_ge(s["m3"].h, M[("mid", it)])
                    s["m1"].inc(t_.matmul(pmid[:, :], C.onesFr[:],
                                          stt[0:1, 8:12], start=True, stop=True))
                    t_.wait_ge(s["m3"].h, M[("msk", it)])
                    s["m1"].inc(t_.matmul(pcnt[:, :], C.onesB128[:],
                                          mskb[:, :, :], start=True, stop=True))
                t_.wait_ge(s["m3"].h, M[("fin", NIT - 1)])
                s["m1"].inc(t_.matmul(pmid[:, :], C.onesFr[:], stt[0:1, 4:8],
                                      start=True, stop=True))
                t_.wait_ge(s["m3"].h, M["hic"])
                s["m1"].inc(t_.matmul(pmid[:, :], C.onesFr[:], stt[0:1, 0:4],
                                      start=True, stop=True))
                t_.wait_ge(s["m3"].h, M["selDb"])
                s["m1"].inc(t_.matmul(pcnt[:, :], C.onesB128[:], mskb[:, :, :],
                                      start=True, stop=True))
                t_.wait_ge(s["m3"].h, M["need"])
                s["m1"].inc(t_.matmul(pmid[:, :], C.onesFr[:], stt[0:1, 20:24],
                                      start=True, stop=True))
                for mk in ("eqb", "selb"):
                    t_.wait_ge(s["m3"].h, M[mk])
                    s["m1"].inc(t_.matmul(pcnt[:, :], C.onesB128[:],
                                          mskb[:, :, :], start=True, stop=True))
                    t_.wait_ge(s["m3"].h, M["pref_" + mk])
                    s["m1"].inc(t_.matmul(pbcA[:, :], C.onesFr[:],
                                          pfb1[0:1, :, :].rearrange(
                                              "one b c -> one (b c)"),
                                          start=True, stop=True))
                    s["m1"].inc(t_.matmul(ptrl[:, :], C.tril[:], mskb[:, :, :],
                                          start=True, stop=True))
                for b in range(4):
                    for st2 in range(8):
                        for c in range(32):
                            t_.wait_ge(s["m3"].h, M[("mt", b, st2, c)])
                            if st2 == 0 and c == 0 and b >= 1:
                                t_.wait_ge(s["m3"].h, M[("ev", b - 1)])
                            ins = t_.matmul(pmeta[:, st2, :], Mt[:, c % 2, st2, :],
                                            metac[:, b, c, :],
                                            start=(c == 0), stop=(c == 31))
                            s["m1"].inc(ins)

            @blk.gpsimd
            def _(g):
                g.wait_ge(s["gp1"].h, bgp)
                s["gp1"].inc(g.memset(stt[0:1, 0:4], 0.0))
                g.wait_ge(s["gp1"].h, s["gp1"].n)
                s["gp1"].inc(g.memset(stt[0:1, 4:8], 2.0))
                g.wait_ge(s["gp1"].h, s["gp1"].n)


def ph_moe_ffn(C, b):
    nc, s = C.nc, C.s
    with ExitStack() as stb:
        eb = stb.enter_context
        xinT = eb(nc.sbuf_tensor(f"f{b}_xT", [128, 8, 1024], BF16))
        hT = eb(nc.sbuf_tensor(f"f{b}_hT", [128, 32, 1024], BF16))
        yT = eb(nc.sbuf_tensor(f"f{b}_yT", [128, 8, 1024], BF16))

        with ExitStack() as st:
            e = st.enter_context
            xg = e(nc.sbuf_tensor(f"f{b}_xg", [128, 8, 1024], BF16))
            px = [e(nc.psum_tensor(f"f{b}_px{k}", [128, 1024], F32)) for k in (0, 1)]
            with nc.Block() as blk:
                bm3, bm1 = s["m3"].n, s["m1"].n
                bq = [s["q0"].n, s["q1"].n]

                @blk.vector
                def _(v):
                    for t in range(8):
                        v.wait_ge(s["m1"].h, bm1 + 8 * (t + 1))
                        s["m3"].inc(v.tensor_copy(xinT[:, 0:4, 128 * t:128 * (t + 1)],
                                                  px[t % 2][:, 0:512]))
                        s["m3"].inc(v.tensor_copy(xinT[:, 4:8, 128 * t:128 * (t + 1)],
                                                  px[t % 2][:, 512:1024]))
                        v.wait_ge(s["m3"].h, s["m3"].n)

                @blk.gpsimd
                def _(g):
                    g.wait_ge(s["q0"].h, bq[0])
                    g.wait_ge(s["q1"].h, bq[1])
                    for t in range(8):
                        if t >= 2:
                            g.wait_ge(s["q%d" % (t % 2)].h, bq[t % 2] + 16 * (t // 2))
                        s["q%d" % (t % 2)].inc(g.indirect_dma_start(
                            out=xg[:, t, :], out_offset=None,
                            in_=C.xnall.rearrange("b s d -> (b s) d"),
                            in_offset=bass.IndirectOffsetOnAxis(
                                ap=C.tokuall[:, b, t:t + 1], axis=0)), 16)

                @blk.tensor
                def _(t_):
                    for t in range(8):
                        t_.wait_ge(s["q%d" % (t % 2)].h, bq[t % 2] + 16 * (t // 2 + 1))
                        if t >= 2:
                            t_.wait_ge(s["m3"].h, bm3 + 2 * (t - 2) + 2)
                        for o in range(8):
                            ins = t_.matmul(px[t % 2][:, 128 * o:128 * (o + 1)],
                                            xg[:, t, 128 * o:128 * (o + 1)],
                                            C.ident[:], start=True, stop=True)
                            s["m1"].inc(ins)

        with ExitStack() as st:
            e = st.enter_context
            wlg = e(nc.sbuf_tensor(f"f{b}_wg", [128, 2, 8, 128], BF16))
            wlu = e(nc.sbuf_tensor(f"f{b}_wu", [128, 2, 8, 128], BF16))
            sg = e(nc.sbuf_tensor(f"f{b}_sg", [128, 512], BF16))
            su = e(nc.sbuf_tensor(f"f{b}_su", [128, 512], BF16))
            pg = [e(nc.psum_tensor(f"f{b}_pg{k}", [128, 512], F32)) for k in (0, 1)]
            pu = [e(nc.psum_tensor(f"f{b}_pu{k}", [128, 512], F32)) for k in (0, 1)]
            with nc.Block() as blk:
                bq = [s["q0"].n, s["q1"].n]
                bpe2, bac1, bve1 = s["pe2"].n, s["ac1"].n, s["ve1"].n

                @blk.gpsimd
                def _(g):
                    g.wait_ge(s["q0"].h, bq[0])
                    g.wait_ge(s["q1"].h, bq[1])
                    for o in range(32):
                        if o >= 2:
                            g.wait_ge(s["pe2"].h, bpe2 + 4 * (o - 1))
                        s["q%d" % (o % 2)].inc(g.dma_start(
                            out=wlg[:, o % 2, :, :], in_=C.wgt[o]), 16)
                        s["q%d" % (o % 2)].inc(g.dma_start(
                            out=wlu[:, o % 2, :, :], in_=C.wut[o]), 16)

                @blk.tensor
                def _(t_):
                    for o in range(32):
                        t_.wait_ge(s["q%d" % (o % 2)].h, bq[o % 2] + 32 * (o // 2 + 1))
                        for tc in range(2):
                            ch = 2 * o + tc
                            if ch >= 2:
                                t_.wait_ge(s["ac1"].h, bac1 + 2 * ch - 2)
                            for k in range(8):
                                ins = t_.matmul(pg[ch % 2][:], wlg[:, o % 2, k, :],
                                                xinT[:, k, 512 * tc:512 * (tc + 1)],
                                                start=(k == 0), stop=(k == 7))
                            s["pe2"].inc(ins)
                            for k in range(8):
                                ins = t_.matmul(pu[ch % 2][:], wlu[:, o % 2, k, :],
                                                xinT[:, k, 512 * tc:512 * (tc + 1)],
                                                start=(k == 0), stop=(k == 7))
                            s["pe2"].inc(ins)

                @blk.scalar
                def _(a):
                    for ch in range(64):
                        a.wait_ge(s["pe2"].h, bpe2 + 2 * ch + 2)
                        if ch >= 1:
                            a.wait_ge(s["ve1"].h, bve1 + ch)
                        s["ac1"].inc(a.activation(sg[:, :], pg[ch % 2][:], AF.Silu))
                        s["ac1"].inc(a.activation(su[:, :], pu[ch % 2][:],
                                                  AF.Identity, bias=C.zeroF[:]))

                @blk.vector
                def _(v):
                    for ch in range(64):
                        o, tc = divmod(ch, 2)
                        v.wait_ge(s["ac1"].h, bac1 + 2 * ch + 2)
                        s["ve1"].inc(v.tensor_tensor(
                            hT[:, o, 512 * tc:512 * (tc + 1)], sg[:, :], su[:, :],
                            op=OP.mult))
                        v.wait_ge(s["ve1"].h, s["ve1"].n)

        with ExitStack() as st:
            e = st.enter_context
            wld = e(nc.sbuf_tensor(f"f{b}_wd", [128, 2, 32, 128], BF16))
            pd = [e(nc.psum_tensor(f"f{b}_pd{k}", [128, 512], F32)) for k in (0, 1)]
            with nc.Block() as blk:
                bq = [s["q0"].n, s["q1"].n]
                bpe2, bac1 = s["pe2"].n, s["ac1"].n

                @blk.gpsimd
                def _(g):
                    g.wait_ge(s["q0"].h, bq[0])
                    g.wait_ge(s["q1"].h, bq[1])
                    for od in range(8):
                        if od >= 2:
                            g.wait_ge(s["pe2"].h, bpe2 + 2 * (od - 1))
                        s["q%d" % (od % 2)].inc(g.dma_start(
                            out=wld[:, od % 2, :, :], in_=C.wdt[od]), 16)

                @blk.tensor
                def _(t_):
                    for od in range(8):
                        t_.wait_ge(s["q%d" % (od % 2)].h, bq[od % 2] + 16 * (od // 2 + 1))
                        for tc in range(2):
                            ch = 2 * od + tc
                            if ch >= 2:
                                t_.wait_ge(s["ac1"].h, bac1 + ch - 1)
                            for k in range(32):
                                ins = t_.matmul(pd[ch % 2][:], wld[:, od % 2, k, :],
                                                hT[:, k, 512 * tc:512 * (tc + 1)],
                                                start=(k == 0), stop=(k == 31))
                            s["pe2"].inc(ins)

                @blk.scalar
                def _(a):
                    for ch in range(16):
                        od, tc = divmod(ch, 2)
                        a.wait_ge(s["pe2"].h, bpe2 + ch + 1)
                        s["ac1"].inc(a.activation(
                            yT[:, od, 512 * tc:512 * (tc + 1)], pd[ch % 2][:],
                            AF.Identity, bias=C.zeroF[:]))

        with ExitStack() as st:
            e = st.enter_context
            ytl = [e(nc.sbuf_tensor(f"f{b}_yt{k}", [128, 1024], BF16)) for k in (0, 1)]
            py = [e(nc.psum_tensor(f"f{b}_py{k}", [128, 1024], F32)) for k in (0, 1)]
            with nc.Block() as blk:
                bm1, bve1 = s["m1"].n, s["ve1"].n
                bq = [s["q0"].n, s["q1"].n]

                @blk.tensor
                def _(t_):
                    for t in range(8):
                        if t >= 2:
                            t_.wait_ge(s["ve1"].h, bve1 + t - 1)
                        for od in range(8):
                            ins = t_.matmul(py[t % 2][:, 128 * od:128 * (od + 1)],
                                            yT[:, od, 128 * t:128 * (t + 1)],
                                            C.ident[:], start=True, stop=True)
                            if od == 7:
                                s["m1"].inc(ins)

                @blk.vector
                def _(v):
                    for t in range(8):
                        v.wait_ge(s["m1"].h, bm1 + t + 1)
                        if t >= 2:
                            v.wait_ge(s["q%d" % (t % 2)].h, bq[t % 2] + 16 * (t // 2))
                        s["ve1"].inc(v.tensor_scalar(ytl[t % 2][:], py[t % 2][:],
                                                     C.gall[:, b, t:t + 1], None,
                                                     op0=OP.mult))
                        v.wait_ge(s["ve1"].h, s["ve1"].n)

                @blk.gpsimd
                def _(g):
                    g.wait_ge(s["q0"].h, bq[0])
                    g.wait_ge(s["q1"].h, bq[1])
                    g.wait_ge(s["w0"].h, C.zdone)
                    for t in range(8):
                        g.wait_ge(s["ve1"].h, bve1 + t + 1)
                        if t >= 2:
                            g.wait_ge(s["q%d" % (t % 2)].h, bq[t % 2] + 16 * (t // 2))
                        s["q%d" % (t % 2)].inc(g.indirect_dma_start(
                            out=C.partial.rearrange("b s d -> (b s) d"),
                            out_offset=bass.IndirectOffsetOnAxis(
                                ap=C.tokuall[:, b, t:t + 1], axis=0),
                            in_=ytl[t % 2][:], in_offset=None,
                            bounds_check=16383, oob_is_err=False), 16)
                    g.wait_ge(s["q0"].h, s["q0"].n)
                    g.wait_ge(s["q1"].h, s["q1"].n)


def build_l1d():
    nc = bass.Bass("TRN2", target_bir_lowering=False)
    C = Ctx()
    C.nc, C.stack = nc, []

    def ent(cm):
        C.stack.append(cm)
        return cm.__enter__()

    C.ent = ent
    C.x1own = nc.dram_tensor("x1own", [OWN, D], F32, kind="ExternalInput").ap()
    C.s1chD = nc.dram_tensor("s1chD", [128, 8, OWN], BF16, kind="ExternalInput").ap()
    C.zstD = nc.dram_tensor("zstD", [16, 4, 512], F32, kind="ExternalInput").ap()
    C.parts = nc.dram_tensor("parts", [16, 8, 128, D], BF16, kind="ExternalInput").ap()
    C.identD = nc.dram_tensor("identD", [128, 128], BF16, kind="ExternalInput").ap()
    C.outown = nc.dram_tensor("outown", [OWN, D], F32, kind="ExternalOutput").ap()

    C.ident = ent(nc.sbuf_tensor("ident", [128, 128], BF16))
    C.identF = ent(nc.sbuf_tensor("identF", [128, 128], F32))
    s1sb = ent(nc.sbuf_tensor("s1sb", [128, 8, OWN], BF16))
    zsb = ent(nc.sbuf_tensor("zsb", [16, 4, 512], F32))
    psm = [ent(nc.sbuf_tensor(f"psm{k}", [128, 8, D], BF16)) for k in (0, 1)]
    mt1 = ent(nc.sbuf_tensor("mt1", [128, 4, D], BF16))
    mt3 = ent(nc.sbuf_tensor("mt3", [128, D], BF16))
    gf = [ent(nc.sbuf_tensor(f"gf{k}", [128, D], F32)) for k in (0, 1)]
    zr = ent(nc.sbuf_tensor("zr", [128, 16], F32))
    s1n = ent(nc.sbuf_tensor("s1n", [128, D], F32))
    x1t = [ent(nc.sbuf_tensor(f"x1t{k}", [128, D], F32)) for k in (0, 1)]
    osb = [ent(nc.sbuf_tensor(f"osb{k}", [128, D], F32)) for k in (0, 1)]
    ps1 = [ent(nc.psum_tensor(f"ps1_{k}", [128, D], F32)) for k in (0, 1)]
    pz = [ent(nc.psum_tensor(f"pz{k}", [128, 16], F32)) for k in (0, 1)]

    names = ["dA", "pe1", "pe2", "ac1", "ve1", "gp1", "q0", "q1",
             "w0", "w1", "l0", "l1"]
    C.s = {k: Cnt(ent(nc.semaphore("s_" + k))) for k in names}
    s = C.s
    NV = 26

    with nc.Block() as blk:
        @blk.sync
        def _(sy):
            s["dA"].inc(sy.dma_start(out=C.ident[:], in_=C.identD), 16)
            s["dA"].inc(sy.dma_start(out=s1sb[:, :, :], in_=C.s1chD), 16)
            s["dA"].inc(sy.dma_start(out=zsb[:, :, :], in_=C.zstD), 16)

        @blk.vector
        def _(v):
            v.wait_ge(s["dA"].h, 48)
            s["ve1"].inc(v.tensor_copy(C.identF[:], C.ident[:]))
            v.wait_ge(s["ve1"].h, s["ve1"].n)

    with nc.Block() as blk:
        bl = [s["l0"].n, s["l1"].n]
        bq = [s["q0"].n, s["q1"].n]
        bw = [s["w0"].n, s["w1"].n]
        bpe1, bpe2, bac1, bve1 = s["pe1"].n, s["pe2"].n, s["ac1"].n, s["ve1"].n

        @blk.sync
        def _(sy):
            for t in range(16):
                if t >= 2:
                    sy.wait_ge(s["ve1"].h, bve1 + NV * (t - 2) + 7)
                s["l%d" % (t % 2)].inc(sy.dma_start(
                    out=psm[t % 2][:, :, :],
                    in_=C.parts[t].rearrange("e p f -> p e f")), 16)
                if t >= 2:
                    sy.wait_ge(s["ve1"].h, bve1 + NV * (t - 2) + 26)
                s["q%d" % (t % 2)].inc(sy.dma_start(
                    out=x1t[t % 2][:], in_=C.x1own[128 * t:128 * (t + 1), :]), 16)
                sy.wait_ge(s["ve1"].h, bve1 + NV * t + 26)
                s["w%d" % (t % 2)].inc(sy.dma_start(
                    out=C.outown[128 * t:128 * (t + 1), :], in_=osb[t % 2][:]), 16)
            sy.wait_ge(s["w0"].h, s["w0"].n)
            sy.wait_ge(s["w1"].h, s["w1"].n)

        @blk.tensor
        def _(t_):
            for t in range(16):
                if t >= 2:
                    t_.wait_ge(s["ve1"].h, bve1 + NV * (t - 2) + 24)
                for h2 in range(8):
                    ins = t_.matmul(ps1[t % 2][:, 128 * h2:128 * (h2 + 1)],
                                    s1sb[:, h2, 128 * t:128 * (t + 1)],
                                    C.ident[:], start=True, stop=True)
                    s["pe1"].inc(ins)
                if t >= 2:
                    t_.wait_ge(s["ve1"].h, bve1 + NV * (t - 2) + 8)
                s["pe2"].inc(t_.matmul(pz[t % 2][:, :],
                                       zsb[0:16, :, :].rearrange(
                                           "z a c -> z (a c)")[0:16,
                                           128 * t:128 * (t + 1)],
                                       C.identF[0:16, 0:16],
                                       start=True, stop=True))

        @blk.scalar
        def _(a):
            for t in range(16):
                a.wait_ge(s["ve1"].h, bve1 + NV * t + 7)
                if t >= 2:
                    a.wait_ge(s["ve1"].h, bve1 + NV * (t - 2) + 25)
                s["ac1"].inc(a.activation(gf[t % 2][:], mt3[:], AF.Silu))

        @blk.vector
        def _(v):
            def V(ins):
                s["ve1"].inc(ins)
                v.wait_ge(s["ve1"].h, s["ve1"].n)

            for t in range(16):
                v.wait_ge(s["l%d" % (t % 2)].h, bl[t % 2] + 16 * (t // 2 + 1))
                p2 = psm[t % 2].rearrange("p e f -> p e f")
                for j in range(4):
                    V(v.tensor_tensor(mt1[:, j, :], p2[:, 2 * j, :],
                                      p2[:, 2 * j + 1, :], op=OP.add))
                V(v.tensor_tensor(mt1[:, 0, :], mt1[:, 0, :], mt1[:, 1, :],
                                  op=OP.add))
                V(v.tensor_tensor(mt1[:, 2, :], mt1[:, 2, :], mt1[:, 3, :],
                                  op=OP.add))
                V(v.tensor_tensor(mt3[:, :], mt1[:, 0, :], mt1[:, 2, :],
                                  op=OP.add))
                v.wait_ge(s["pe2"].h, bpe2 + t + 1)
                V(v.reciprocal(zr[:, :], pz[t % 2][:, :]))
                v.wait_ge(s["pe1"].h, bpe1 + 8 * (t + 1))
                for h2 in range(16):
                    V(v.tensor_scalar(s1n[:, 64 * h2:64 * (h2 + 1)],
                                      ps1[t % 2][:, 64 * h2:64 * (h2 + 1)],
                                      zr[:, h2:h2 + 1], None, op0=OP.mult))
                v.wait_ge(s["ac1"].h, bac1 + t + 1)
                V(v.tensor_tensor(s1n[:, :], s1n[:, :], gf[t % 2][:],
                                  op=OP.mult))
                v.wait_ge(s["q%d" % (t % 2)].h, bq[t % 2] + 16 * (t // 2 + 1))
                if t >= 2:
                    v.wait_ge(s["w%d" % (t % 2)].h, bw[t % 2] + 16 * (t // 2))
                V(v.tensor_tensor(osb[t % 2][:], s1n[:, :], x1t[t % 2][:],
                                  op=OP.add))
    finish(C)
    return nc, C


def kernel(**inputs):
    import ml_dtypes
    from concourse.bass_utils import run_bass_kernel_spmd
    BFN = ml_dtypes.bfloat16
    x = inputs["L0_attn_x"]
    CHQ = {0: [0, 3, 4, 7], 1: [1, 2, 5, 6]}
    own_rows = {p: np.concatenate([np.arange(c * 512, c * 512 + 512)
                                   for c in CHQ[p]]) for p in (0, 1)}

    perm = perm_rows()
    wqkvT = np.ascontiguousarray(inputs["L0_attn_Wqkv"][perm].T)
    wA = dict(
        wqkvT=np.ascontiguousarray(
            wqkvT.reshape(8, 128, 24, 128).transpose(2, 1, 0, 3)),
        bqkv=np.ascontiguousarray(inputs["L0_attn_bqkv"][perm].reshape(24, 128).T),
        wpreT=np.ascontiguousarray(inputs["L0_attn_Wpre"].T),
        wpostT=np.ascontiguousarray(inputs["L0_attn_Wpost"].T),
        woutT=np.ascontiguousarray(inputs["L0_attn_Wout"].T),
        boutc=np.ascontiguousarray(inputs["L0_attn_bout"].reshape(8, 128).T),
        wfcT=np.ascontiguousarray(inputs["L0_ffn_W_fc"].T),
        lnwc=np.ascontiguousarray(inputs["L0_ffn_ln_w"].reshape(32, 128).T),
        lnbc=np.ascontiguousarray(inputs["L0_ffn_ln_b"].reshape(32, 128).T),
        wprojT=np.ascontiguousarray(inputs["L0_ffn_W_proj"].T),
    )
    hc = {p: host_consts(p * OWN) for p in (0, 1)}
    inA = []
    for c in range(8):
        b, p = c // 2, c % 2
        inA.append(dict(xb=np.ascontiguousarray(np.roll(x[b], -p * OWN, axis=0)),
                        **wA, **hc[p]))
    ncA, _ = build_l0()
    resA = run_bass_kernel_spmd(ncA, inA, core_ids=list(range(8)))
    x1own = [resA.results[c]["x1own"] for c in range(8)]
    x1 = [np.concatenate([x1own[2 * b], x1own[2 * b + 1]], axis=0)
          for b in range(4)]

    wqkv1t = np.ascontiguousarray(
        inputs["L1_attn_W_qkv"].reshape(8, 128, 24, 128).transpose(2, 1, 0, 3))
    gateT = np.ascontiguousarray(inputs["L1_ffn_gate_w"].T)
    identD = np.eye(128, dtype=BFN)
    inB = []
    for c in range(8):
        b, p = c // 2, c % 2
        posr = own_rows[p].reshape(4, 512).astype(np.float32)
        inB.append(dict(x1f=x1[b],
                        x1q=np.ascontiguousarray(x1[b][own_rows[p]]),
                        wqkv1t=wqkv1t, gateT=gateT, posr=posr, identD=identD))
    ncB, _ = build_l1a()
    resB = run_bass_kernel_spmd(ncB, inB, core_ids=list(range(8)))

    xnall = np.stack([resB.results[2 * b]["xn1full"] for b in range(4)])
    affall = np.stack([resB.results[2 * b]["afful"] for b in range(4)])
    trilD = np.tril(np.ones((128, 128), np.float32), -1).astype(BFN)
    inC = []
    for e in range(8):
        wg, wu, wd = (inputs["L1_ffn_w_gate"][e], inputs["L1_ffn_w_up"][e],
                      inputs["L1_ffn_w_down"][e])
        inC.append(dict(
            affs=np.ascontiguousarray(affall[:, :, e]),
            xnall=xnall,
            wgt=np.ascontiguousarray(
                wg.T.reshape(8, 128, 32, 128).transpose(2, 1, 0, 3)),
            wut=np.ascontiguousarray(
                wu.T.reshape(8, 128, 32, 128).transpose(2, 1, 0, 3)),
            wdt=np.ascontiguousarray(
                wd.T.reshape(32, 128, 8, 128).transpose(2, 1, 0, 3)),
            identD=identD, trilD=trilD))
    ncC, _ = build_l1b()
    resC = run_bass_kernel_spmd(ncC, inC, core_ids=list(range(8)))
    partial = [resC.results[e]["partial"] for e in range(8)]

    inD = []
    for c in range(8):
        b, p = c // 2, c % 2
        rows = own_rows[p]
        parts = np.stack([
            np.stack([partial[e][b, rows[128 * t:128 * (t + 1)], :]
                      for e in range(8)]) for t in range(16)])
        inD.append(dict(x1own=np.ascontiguousarray(x1[b][rows]),
                        s1chD=resB.results[c]["s1ch"],
                        zstD=resB.results[c]["zst"],
                        parts=np.ascontiguousarray(parts),
                        identD=identD))
    ncD, _ = build_l1d()
    resD = run_bass_kernel_spmd(ncD, inD, core_ids=list(range(8)))

    out = np.empty((4, S, D), np.float32)
    for c in range(8):
        b, p = c // 2, c % 2
        out[b, own_rows[p]] = resD.results[c]["outown"]
    return out
